# version 14
# baseline (speedup 1.0000x reference)
import contextlib
import functools
import numpy as np
import ml_dtypes
import concourse.bass as bass
import concourse.mybir as mybir
from concourse.bass_utils import run_bass_kernel_spmd

F32 = mybir.dt.float32
BF16 = mybir.dt.bfloat16
AF = mybir.ActivationFunctionType
ALU = mybir.AluOpType
BF = ml_dtypes.bfloat16

NCORE = 8
SEQ = 16384
D = 1024
T = SEQ // NCORE
FC = 8
NT = 4
NCH = 16
NP = 8
DH = 4096
HC = 32
LN_EPS = 1e-5
ATT_EPS = 1e-6
INVD = 1.0 / D

SEMS = ("pe", "act", "dve", "cc", "pool",
        "sx0", "sx1", "sx2", "sx3", "sc", "sbk", "swk", "swv", "swq", "swo",
        "w1a", "w1b", "w1c", "w1d", "w2a", "w2b", "g1s", "g2s",
        "lnb", "out", "c1s", "c2s", "cws")


def build_program():
    nc = bass.Bass("TRN2", target_bir_lowering=False)

    xin = nc.dram_tensor("xin", [D, T], F32, kind="ExternalInput")
    wq = nc.dram_tensor("wq", [D, D], F32, kind="ExternalInput")
    wk = nc.dram_tensor("wk", [D, D], F32, kind="ExternalInput")
    wv = nc.dram_tensor("wv", [D, D], F32, kind="ExternalInput")
    wo = nc.dram_tensor("wo", [D, D], F32, kind="ExternalInput")
    w1 = nc.dram_tensor("w1", [D, DH], F32, kind="ExternalInput")
    w2 = nc.dram_tensor("w2", [DH, D], F32, kind="ExternalInput")
    bq = nc.dram_tensor("bq", [128, 8], F32, kind="ExternalInput")
    bo = nc.dram_tensor("bo", [128, 8], F32, kind="ExternalInput")
    b1 = nc.dram_tensor("b1", [128, 32], F32, kind="ExternalInput")
    b2 = nc.dram_tensor("b2", [128, 8], F32, kind="ExternalInput")
    bkv = nc.dram_tensor("bkv", [1, 2 * D], F32, kind="ExternalInput")
    decbc = nc.dram_tensor("decbc", [128, T], BF16, kind="ExternalInput")
    deccol = nc.dram_tensor("deccol", [128, NCH], F32, kind="ExternalInput")
    ident = nc.dram_tensor("ident", [128, 128], BF16, kind="ExternalInput")
    onesin = nc.dram_tensor("onesin", [128, 128], BF16, kind="ExternalInput")
    epsin = nc.dram_tensor("epsin", [128, 1], F32, kind="ExternalInput")
    yout = nc.dram_tensor("y", [D, T], F32, kind="ExternalOutput")

    rowb = nc.dram_tensor("rowb", [4, 1024], F32)
    w1bf = nc.dram_tensor("w1bf", [D, DH], BF16)
    w2bf = nc.dram_tensor("w2bf", [DH, D], BF16)
    wobf = nc.dram_tensor("wobf", [D, D], BF16)
    cc_in = nc.dram_tensor("cc_in", [128, 1040], BF16)
    cc_out = nc.dram_tensor("cc_out", [128, 1040], BF16)

    SCHED = {k: [] for k in ("sync", "gpsimd", "tensor", "vector", "scalar")}
    CNT = {k: 0 for k in SEMS}

    WAITED = {k: {} for k in SCHED}

    def emit(eng, fn, waits=(), inc=None, amt=1):
        wd = WAITED[eng]
        kept = []
        for sm, v in waits:
            if v > 0 and v > wd.get(sm, 0):
                kept.append((sm, v))
                wd[sm] = v
        SCHED[eng].append((fn, tuple(kept), inc, amt))
        if inc is not None:
            CNT[inc] += amt
        return CNT[inc] if inc is not None else None

    def ev(fn, waits=()):
        return emit("vector", fn, list(waits) + [("dve", CNT["dve"])], "dve")

    def ea(fn, waits=()):
        return emit("scalar", fn, list(waits) + [("act", CNT["act"])], "act")

    def et(fn, waits=(), inc=False):
        return emit("tensor", fn, waits, "pe" if inc else None)

    def es(fn, sem, waits=()):
        return emit("sync", fn, waits, sem, 16)

    def eg(fn, sem, waits=()):
        return emit("gpsimd", fn, waits, sem, 16)

    def ep(fn, waits=()):
        return emit("gpsimd", fn,
                    list(waits) + [("pool", CNT["pool"])], "pool")

    with contextlib.ExitStack() as ctx:
        en = ctx.enter_context

        x_sb = en(nc.sbuf_tensor("x_sb", [128, FC * T], F32))
        bfA = en(nc.sbuf_tensor("bfA", [128, FC * T], BF16))
        qh = en(nc.sbuf_tensor("qh", [128, FC * T], BF16))
        wa = en(nc.sbuf_tensor("wa", [128, 16384], BF16))
        tmp = en(nc.sbuf_tensor("tmp", [128, 512], F32))
        bc = en(nc.sbuf_tensor("bc", [128, 1024], F32))
        rows = en(nc.sbuf_tensor("rows", [128, 1536], F32))
        dcol = en(nc.sbuf_tensor("dcol", [128, NCH], F32))
        xbf = en(nc.sbuf_tensor("xbf", [128, 1024], BF16))
        sqb = en(nc.sbuf_tensor("sqb", [128, 1024], BF16))
        kvacc = en(nc.sbuf_tensor("kvacc", [128, 1040], F32))
        kvbd = en(nc.sbuf_tensor("kvbd", [128, 1040], BF16))
        recip = en(nc.sbuf_tensor("recip", [128, 16], F32))
        id_sb = en(nc.sbuf_tensor("id_sb", [128, 128], BF16))
        on_sb = en(nc.sbuf_tensor("on_sb", [128, 128], BF16))
        bq_sb = en(nc.sbuf_tensor("bq_sb", [128, 8], F32))
        bo_sb = en(nc.sbuf_tensor("bo_sb", [128, 8], F32))
        b1_sb = en(nc.sbuf_tensor("b1_sb", [128, 32], F32))
        b2_sb = en(nc.sbuf_tensor("b2_sb", [128, 8], F32))
        eps_sb = en(nc.sbuf_tensor("eps_sb", [128, 1], F32))
        par = en(nc.sbuf_tensor("par", [128, 2064], F32))
        yA = en(nc.sbuf_tensor("yA", [128, 4096], F32))

        parb = par.ap().bitcast(BF16)
        yab = yA.ap().bitcast(BF16)
        bkv_sb = yA.ap().bitcast(BF16)[0:1, 0:2 * D]
        decbc_v = parb[:, 0:2048]
        kfr_s = [parb[:, 2048:3072], None]
        vr_s = [parb[:, 3072:4112], None]
        xn2_v = parb[:, 0:4096]
        odv_v = kvacc.ap().bitcast(BF16)[:, 0:1024]
        kfr_s[1] = kvbd.ap()[:, 0:1024]
        vr_s[1] = yab[:, 2048:3088]
        tmp2 = kvacc.ap()[:, 512:1024]

        P = [en(nc.psum_tensor(f"P{i}", [128, 512], F32)) for i in range(8)]
        TRB = [P[6].ap().bitcast(BF16)[:, 0:128],
               P[7].ap().bitcast(BF16)[:, 0:128]]
        pw_act = [0] * 8
        pw_dve = [0] * 8

        SEM = {}
        for s in SEMS:
            SEM[s] = en(nc.semaphore(s))


        for t in range(NT):
            def f(eng, t=t):
                return eng.dma_start(
                    out=x_sb.ap().rearrange("p (c n) -> p c n", c=FC)[
                        :, :, 512 * t:512 * (t + 1)],
                    in_=xin.ap().rearrange("(c p) n -> p c n", p=128)[
                        :, :, 512 * t:512 * (t + 1)])
            es(f, "sx%d" % t)
        XL = [("sx%d" % t, 16) for t in range(NT)]

        for dst, src in ((dcol, deccol), (id_sb, ident), (on_sb, onesin),
                         (bq_sb, bq), (bo_sb, bo), (b1_sb, b1), (b2_sb, b2),
                         (eps_sb, epsin)):
            def f(eng, dst=dst, src=src):
                return eng.dma_start(out=dst[:, :], in_=src[:, :])
            es(f, "sc")

        def f(eng):
            return eng.dma_start(out=decbc_v, in_=decbc[:, :])
        es(f, "sc")
        LDC = ("sc", CNT["sc"])

        eg(lambda eng: eng.dma_start(out=bkv_sb, in_=bkv[:, :]), "sbk")

        def wload(eng, dst0, src):
            return eng.dma_start(
                out=wa.ap()[:, dst0:dst0 + 8192].rearrange(
                    "p (c o) -> p c o", c=FC),
                in_=src.ap().rearrange("(c p) o -> p c o", p=128))
        eg(functools.partial(wload, dst0=0, src=wk), "swk")
        eg(functools.partial(wload, dst0=8192, src=wv), "swv")

        def wqload(eng):
            return eng.dma_start(
                out=qh.ap()[:, 8192:16384].rearrange(
                    "p (a c o) -> p a c o", a=FC, c=FC),
                in_=wq.ap().rearrange("(c p) (a o) -> p a c o",
                                      p=128, o=128))
        eg(wqload, "swq")

        eg(lambda eng: eng.dma_start(out=w1bf.ap(), in_=w1.ap()), "c1s")
        eg(lambda eng: eng.dma_start(out=w2bf.ap(), in_=w2.ap()), "c2s")
        eg(lambda eng: eng.dma_start(out=wobf.ap(), in_=wo.ap()), "cws")

        ev(lambda eng: eng.memset(
            vr_s[0].rearrange("p (s c) -> p s c", c=130)[:, :, 128:130], 1.0))
        d_ones = ev(lambda eng: eng.memset(
            vr_s[1].rearrange("p (s c) -> p s c", c=130)[:, :, 128:130], 1.0))

        ln_state = {"slot": 0, "bc_reader": 0, "bc_reader_pool": 0,
                    "row_war": 0, "stats_read": 0}
        xbf_war = [0, 0]

        def ln_stats(src_ap, extra_waits=(), keep=None, fc_wait=None):
            extra = list(extra_waits)
            slot_guard = [xbf_war[0], xbf_war[1]]
            pe_sum = pe_sq = 0
            cast_done = 0
            for fc in range(FC):
                sl = fc % 2
                dst = keep(fc) if keep is not None else                     xbf[:, 512 * sl:512 * sl + 512]
                cast = ev(lambda eng, fc=fc, dst=dst: eng.tensor_copy(
                    dst, src_ap(fc)),
                    (extra if fc <= 1 else [])
                    + (fc_wait(fc) if fc_wait is not None else [])
                    + ([("pe", slot_guard[sl])] if keep is None else []))
                cast_done = cast
                sq = ea(lambda eng, dst=dst, sl=sl: eng.activation(
                    sqb[:, 512 * sl:512 * sl + 512], dst, AF.Square),
                    [("dve", cast), ("pe", slot_guard[sl]), LDC])
                w_mm = [("dve", cast)]
                if fc == 0:
                    w_mm += extra + [("dve", ln_state["stats_read"]),
                                     ("act", pw_act[6]), ("dve", pw_dve[6]),
                                     LDC]
                et(lambda eng, dst=dst, fc=fc: eng.matmul(
                    P[6][0:1, :], on_sb[:, 0:1], dst,
                    start=(fc == 0), stop=(fc == FC - 1)), w_mm)
                w_mm2 = [("act", sq)]
                if fc == 0:
                    w_mm2 += extra + [("dve", ln_state["stats_read"]),
                                      ("act", pw_act[7]), ("dve", pw_dve[7])]
                r = et(lambda eng, sl=sl, fc=fc: eng.matmul(
                    P[7][0:1, :], on_sb[:, 0:1],
                    sqb[:, 512 * sl:512 * sl + 512],
                    start=(fc == 0), stop=(fc == FC - 1)), w_mm2, inc=True)
                slot_guard[sl] = r
                if fc == FC - 1:
                    pe_sum = r
                    pe_sq = r
            xbf_war[0] = slot_guard[0]
            xbf_war[1] = slot_guard[1]
            ev(lambda eng: eng.tensor_scalar(
                rows[0:1, 0:512], P[6][0:1, :], -INVD, None, ALU.mult),
                [("pe", pe_sum), ("lnb", ln_state["row_war"])])
            ev(lambda eng: eng.tensor_scalar(
                rows[0:1, 1024:1536], P[7][0:1, :], INVD, None, ALU.mult),
                [("pe", pe_sq)])
            ev(lambda eng: eng.tensor_tensor(
                rows[0:1, 512:1024], rows[0:1, 0:512], rows[0:1, 0:512],
                ALU.mult))
            d_var = ev(lambda eng: eng.tensor_tensor(
                rows[0:1, 1024:1536], rows[0:1, 1024:1536],
                rows[0:1, 512:1024], ALU.subtract))
            ln_state["stats_read"] = d_var
            pw_dve[6] = d_var
            pw_dve[7] = d_var
            ea(lambda eng: eng.activation(
                rows[0:1, 1024:1536], rows[0:1, 1024:1536], AF.Ln,
                bias=eps_sb[0:1, :]), [("dve", d_var), LDC])
            a_exp = ea(lambda eng: eng.activation(
                rows[0:1, 512:1024], rows[0:1, 1024:1536], AF.Exp,
                scale=-0.5))
            slot = ln_state["slot"] % 4
            ln_state["slot"] += 1
            b_out = es(lambda eng, slot=slot: eng.dma_start(
                out=rowb[slot:slot + 1, :], in_=rows[0:1, 0:1024]),
                "lnb", [("act", a_exp), ("lnb", ln_state["row_war"])])
            ln_state["row_war"] = b_out
            bc_in = es(lambda eng, slot=slot: eng.dma_start(
                out=bc[:, 0:1024],
                in_=rowb[slot:slot + 1, :].partition_broadcast(128)),
                "lnb", [("lnb", b_out), ("dve", ln_state["bc_reader"]),
                        ("pool", ln_state["bc_reader_pool"])])
            return bc_in, cast_done

        xn_done = [0] * NT

        def ln1_stats(t):
            return ln_stats(
                lambda fc, t=t: x_sb[:, T * fc + 512 * t:
                                     T * fc + 512 * t + 512],
                extra_waits=[XL[t]])[0]

        def ln1_apply(t, bc_in):
            for fc in range(FC):
                ev(lambda eng, fc=fc, t=t: eng.tensor_tensor(
                    tmp[:, :], x_sb[:, T * fc + 512 * t:
                                    T * fc + 512 * t + 512],
                    bc[:, 0:512], ALU.add), [("lnb", bc_in)])
                ev(lambda eng, fc=fc, t=t: eng.tensor_tensor(
                    bfA[:, T * fc + 512 * t: T * fc + 512 * t + 512],
                    tmp[:, :], bc[:, 512:1024], ALU.mult))
            xn_done[t] = CNT["dve"]
            ln_state["bc_reader"] = CNT["dve"]

        kf_war = [0, 0]
        vr_war = [0, 0]
        kv_mm_last = [0]
        last_k_mm = [0]
        last_v_mm = [0]
        kv_add_last = [0]

        bc0 = ln1_stats(0)
        ln1_apply(0, bc0)
        ln1_pend = [None]

        def kv_chunk(c):
            t = c // 4
            if c % 4 == 2 and t + 1 < NT:
                ln1_pend[0] = ln1_stats(t + 1)
            if c % 4 == 3 and ln1_pend[0] is not None:
                ln1_apply(t + 1, ln1_pend[0])
                ln1_pend[0] = None
            sl_ = c % 2
            kfr_v = kfr_s[sl_]
            vr_v = vr_s[sl_]
            for ot in range(2):
                bank = ot
                for fc in range(FC):
                    w = []
                    if fc == 0:
                        w = [("swk", 16), ("dve", xn_done[t]),
                             ("act", pw_act[bank]), ("dve", pw_dve[bank]),
                             ("sbk", 16), LDC]
                    et(lambda eng, fc=fc, ot=ot, c=c, bank=bank: eng.matmul(
                        P[bank][:, :],
                        bfA[:, T * fc + 128 * c: T * fc + 128 * c + 128],
                        wa[:, 1024 * fc + 512 * ot:
                           1024 * fc + 512 * ot + 512],
                        start=(fc == 0), stop=False), w)
                mmk = et(lambda eng, ot=ot, bank=bank: eng.matmul(
                    P[bank][:, :], on_sb[0:1, 0:128],
                    bkv_sb[0:1, 512 * ot:512 * ot + 512],
                    start=False, stop=True), [], inc=True)
                last_k_mm[0] = mmk
                e = ea(lambda eng, ot=ot, c=c, bank=bank, kfr_v=kfr_v:
                       eng.activation(
                    kfr_v[:, 512 * ot:512 * ot + 512], P[bank][:, :],
                    AF.Relu, scale=dcol[:, c:c + 1]),
                    [("pe", mmk), ("pe", kf_war[sl_]), LDC])
                pw_act[bank] = e
            for ot in range(2):
                bank = 2 + ot
                for fc in range(FC):
                    w = []
                    if fc == 0:
                        w = [("swv", 16), ("dve", xn_done[t]),
                             ("act", pw_act[bank]), ("dve", pw_dve[bank])]
                    et(lambda eng, fc=fc, ot=ot, c=c, bank=bank: eng.matmul(
                        P[bank][:, :],
                        bfA[:, T * fc + 128 * c: T * fc + 128 * c + 128],
                        wa[:, 8192 + 1024 * fc + 512 * ot:
                           8192 + 1024 * fc + 512 * ot + 512],
                        start=(fc == 0), stop=False), w)
                mmv = et(lambda eng, ot=ot, bank=bank: eng.matmul(
                    P[bank][:, :], on_sb[0:1, 0:128],
                    bkv_sb[0:1, D + 512 * ot:D + 512 * ot + 512],
                    start=False, stop=True), [], inc=True)
                last_v_mm[0] = mmv
                e = ea(lambda eng, ot=ot, bank=bank, vr_v=vr_v:
                       eng.activation(
                    vr_v.rearrange("p (s c) -> p s c", c=130)[
                        :, 4 * ot:4 * ot + 4, 0:128],
                    P[bank][:, 0:512].rearrange("p (s c) -> p s c", c=128),
                    AF.Copy),
                    [("pe", mmv), ("pe", vr_war[sl_])])
                pw_act[bank] = e
            kevac_q[c] = CNT["act"]
            if c > 0:
                kv_pairs(c - 1)

        def kv_pairs(c):
            sl_ = c % 2
            kfr_v = kfr_s[sl_]
            vr_v = vr_s[sl_]
            kevac = kevac_q[c]
            for p in range(NP):
                bank = 4 + (p % 2)
                w = [("act", kevac), ("dve", pw_dve[bank])]
                if c <= 1 and p < 2:
                    w.append(("dve", d_ones))
                mm = et(lambda eng, p=p, bank=bank, kfr_v=kfr_v, vr_v=vr_v:
                        eng.matmul(
                    P[bank][:, 0:130],
                    kfr_v[:, 128 * p:128 * p + 128],
                    vr_v[:, 130 * p:130 * p + 130],
                    start=True, stop=True), w, inc=True)
                if c == 0:
                    a = ev(lambda eng, p=p, bank=bank: eng.tensor_copy(
                        kvacc[:, 130 * p:130 * p + 130], P[bank][:, 0:130]),
                        [("pe", mm)])
                else:
                    a = ev(lambda eng, p=p, bank=bank: eng.tensor_tensor(
                        kvacc[:, 130 * p:130 * p + 130],
                        kvacc[:, 130 * p:130 * p + 130],
                        P[bank][:, 0:130], ALU.add), [("pe", mm)])
                pw_dve[bank] = a
                kv_mm_last[0] = mm
            kf_war[sl_] = kv_mm_last[0]
            vr_war[sl_] = kv_mm_last[0]
            kv_add_last[0] = CNT["dve"]

        kevac_q = [0] * NCH
        for c in range(NCH):
            kv_chunk(c)
        kv_pairs(NCH - 1)

        es(functools.partial(wload, dst0=8192, src=wobf), "swo",
           [("pe", last_v_mm[0]), ("cws", 16)])
        cc_cast = ev(lambda eng: eng.tensor_copy(kvbd[:, :], kvacc[:, :]),
                     [("dve", kv_add_last[0]), ("pe", kv_mm_last[0])])
        eg(lambda eng: eng.dma_start(out=cc_in[:, :], in_=kvbd[:, :]),
           "g1s", [("dve", cc_cast)])

        def coll(eng):
            return eng.collective_compute(
                "AllReduce", ALU.add,
                replica_groups=[list(range(NCORE))],
                ins=[cc_in.ap().opt()],
                outs=[cc_out.ap().opt()])
        emit("gpsimd", coll, [("g1s", 16)], "cc", 1)
        eg(lambda eng: eng.dma_start(out=kvbd[:, :], in_=cc_out[:, :]),
           "g2s", [("cc", 1)])

        tmp_war = [CNT["dve"]]
        last_q_mm = [0]
        for oc in range(FC):
            mmq_t = [0] * NT
            for t in range(NT):
                bank = 4 * (oc % 2) + t
                for fc in range(FC):
                    w = []
                    if fc == 0:
                        w = [("swq", 16),
                             ("act", pw_act[bank]), ("dve", pw_dve[bank])]
                    et(lambda eng, fc=fc, oc=oc, t=t, bank=bank: eng.matmul(
                        P[bank][:, :],
                        qh[:, 8192 + 1024 * oc + 128 * fc:
                           8192 + 1024 * oc + 128 * fc + 128],
                        bfA[:, T * fc + 512 * t: T * fc + 512 * t + 512],
                        start=(fc == 0), stop=(fc == FC - 1)), w,
                        inc=(fc == FC - 1))
                mmq_t[t] = CNT["pe"]
            last_q_mm[0] = mmq_t[NT - 1]
            for t in range(NT):
                bank = 4 * (oc % 2) + t
                e = ea(lambda eng, oc=oc, bank=bank: eng.activation(
                    tmp[:, :], P[bank][:, :], AF.Relu,
                    bias=bq_sb[:, oc:oc + 1]),
                    [("pe", mmq_t[NT - 1]), ("dve", tmp_war[0]), LDC])
                m = ev(lambda eng, oc=oc, t=t: eng.tensor_tensor(
                    qh[:, T * oc + 512 * t: T * oc + 512 * t + 512],
                    tmp[:, :], decbc_v[:, 512 * t:512 * t + 512], ALU.mult),
                    [("act", e)])
                tmp_war[0] = m
                pw_act[bank] = e
        qf_done = CNT["dve"]

        kbd3 = kvbd.ap().rearrange("p (s c) -> p s c", c=130)
        ev(lambda eng: eng.memset(kbd3[0:64, :, 64:128], 0.0), [("g2s", 16)])
        ev(lambda eng: eng.memset(kbd3[64:128, :, 0:64], 0.0))
        ev(lambda eng: eng.memset(kbd3[64:128, :, 128:129], 0.0))
        kvbd_done = ev(lambda eng: eng.memset(kbd3[0:64, :, 129:130], 0.0))

        odv_war = [0] * NP
        trev = [[0] * NCH for _ in range(NP)]

        def out_chunk(c):
            for half in range(2):
                mm_out = [0] * 4
                for j in range(4):
                    p = 4 * half + j
                    bank = j
                    w = [("dve", kvbd_done), ("act", pw_act[bank]),
                         ("dve", pw_dve[bank])]
                    mm_out[j] = et(lambda eng, c=c, p=p, bank=bank:
                                   eng.matmul(
                        P[bank][:, 0:130],
                        qh[:, T * p + 128 * c: T * p + 128 * c + 128],
                        kvbd[:, 130 * p:130 * p + 130],
                        start=True, stop=True), w, inc=True)
                for j in range(4):
                    p = 4 * half + j
                    ev(lambda eng, p=p, bank=j: eng.tensor_scalar(
                        recip.ap()[:, 2 * p:2 * p + 2],
                        P[bank][:, 128:130],
                        ATT_EPS, None, ALU.add), [("pe", mm_out[j])])
                ev(lambda eng, half=half: eng.reciprocal(
                    recip[:, 8 * half:8 * half + 8],
                    recip[:, 8 * half:8 * half + 8]))
                for j in range(4):
                    p = 4 * half + j
                    bank = j
                    dvm = ev(lambda eng, p=p, bank=bank: eng.tensor_tensor(
                        odv_v.rearrange("p (s c) -> p s c", c=64)[
                            :, 2 * p:2 * p + 2, :],
                        P[bank][:, 0:128].rearrange(
                            "p (s c) -> p s c", c=64),
                        recip.ap()[:, 2 * p:2 * p + 2].unsqueeze(2)
                        .broadcast_to([128, 2, 64]),
                        ALU.mult), [("pe", odv_war[p])])
                    pw_dve[bank] = dvm
                    tb = p % 2
                    trmm = et(lambda eng, p=p, tb=tb: eng.transpose(
                        TRB[tb], odv_v[:, 128 * p:128 * p + 128],
                        id_sb[:, :]),
                        [("dve", dvm), ("act", pw_act[6 + tb]),
                         ("dve", pw_dve[6 + tb]), LDC], inc=True)
                    odv_war[p] = trmm
                    tev = ea(lambda eng, p=p, c=c, tb=tb: eng.copy(
                        bfA[:, T * p + 128 * c: T * p + 128 * c + 128],
                        TRB[tb]), [("pe", trmm)])
                    pw_act[6 + tb] = tev
                    trev[p][c] = tev

        def w1load(eng, hg, slot):
            return eng.dma_start(
                out=wa.ap()[:, 2048 * slot:2048 * slot + 2048].rearrange(
                    "p (c h) -> p c h", c=FC),
                in_=w1bf.ap().rearrange("(c p) h -> p c h", p=128)[
                    :, :, 256 * hg:256 * hg + 256])

        def w2load(eng, wg, slot):
            return eng.dma_start(
                out=wa.ap()[:, 8192 + 4096 * slot:
                            8192 + 4096 * slot + 4096]
                .rearrange("p (c e) -> p c e", c=4),
                in_=w2bf.ap().rearrange("(c p) e -> p c e", p=128)[
                    :, 4 * wg:4 * wg + 4, :])

        w1_ring_war = [last_k_mm[0]] * 4
        w2_ring_war = [0, 0]
        W1SEM = ("w1a", "w1b", "w1c", "w1d")
        w1_pre = [0] * 4
        for u in range(4):
            w1_pre[u] = es(functools.partial(w1load, hg=u, slot=u),
                           W1SEM[u], [("pe", last_k_mm[0]), ("c1s", 16)])

        xn2_ready_q = [0] * NT
        w1_last_mm = [0]
        ln3_bc = [0]
        x_upd2 = [0] * NT

        def emit_ln3_stats(qt):
            ln3_extra = []
            if qt == 0:
                ln3_extra += [("pe", kv_mm_last[0]), ("dve", qf_done)]
            ln3_bc[0] = ln_stats(
                lambda fc, qt=qt: x_sb[:, T * fc + 512 * qt:
                                       T * fc + 512 * qt + 512],
                extra_waits=ln3_extra,
                fc_wait=lambda fc, qt=qt: [("pool", x_upd_fc[qt][fc])])[0]

        xn2_fc = [0] * FC

        def emit_ln3_apply(qt):
            for fc in range(FC):
                w = [("lnb", ln3_bc[0])]
                if fc == 0 and qt > 0:
                    w.append(("pe", w1_last_mm[0]))
                ev(lambda eng, fc=fc, qt=qt: eng.tensor_tensor(
                    tmp[:, :], x_sb[:, T * fc + 512 * qt:
                                    T * fc + 512 * qt + 512],
                    bc[:, 0:512], ALU.add), w)
                xn2_fc[fc] = ev(lambda eng, fc=fc: eng.tensor_tensor(
                    xn2_v[:, 512 * fc:512 * fc + 512],
                    tmp[:, :], bc[:, 512:1024], ALU.mult))
            xn2_ready_q[qt] = xn2_fc[FC - 1]
            ln_state["bc_reader"] = CNT["dve"]

        yA_war = [0]
        t2_war = [0]
        x_upd_fc = [[0] * FC for _ in range(NT)]
        last_wo_mm = [0]
        for qt in range(NT):
            for c in range(4 * qt, 4 * qt + 4):
                out_chunk(c)
            yev = [0] * FC
            for oc in range(FC):
                bank = oc % 2
                for ec in range(FC):
                    w = [("act", trev[ec][4 * qt + 3])]
                    if ec == 0:
                        w += [("swo", 16), ("act", pw_act[bank]),
                              ("dve", pw_dve[bank])]
                    et(lambda eng, oc=oc, ec=ec, qt=qt, bank=bank:
                       eng.matmul(
                           P[bank][:, :],
                           wa[:, 8192 + 1024 * ec + 128 * oc:
                              8192 + 1024 * ec + 128 * oc + 128],
                           bfA[:, T * ec + 512 * qt:
                               T * ec + 512 * qt + 512],
                           start=(ec == 0), stop=(ec == FC - 1)), w,
                       inc=(ec == FC - 1))
                mmo = CNT["pe"]
                last_wo_mm[0] = mmo
                e = ea(lambda eng, oc=oc, bank=bank: eng.activation(
                    yA[:, 512 * oc:512 * oc + 512], P[bank][:, :],
                    AF.Identity, bias=bo_sb[:, oc:oc + 1]),
                    [("pe", mmo), ("dve", yA_war[0]), LDC])
                pw_act[bank] = e
                yev[oc] = e
            ybf = (lambda fc, qt=qt:
                   bfA[:, T * fc + 512 * qt: T * fc + 512 * qt + 512])
            bc_in, cast_done = ln_stats(
                lambda fc: yA[:, 512 * fc:512 * fc + 512],
                keep=ybf, extra_waits=[("pe", last_wo_mm[0])],
                fc_wait=lambda fc: [("act", yev[fc])])
            yA_war[0] = cast_done
            op = ep if qt < 3 else ev
            for fc in range(FC):
                op(lambda eng, fc=fc, ybf=ybf: eng.tensor_tensor(
                    tmp2, ybf(fc), bc[:, 0:512], ALU.add),
                    [("lnb", bc_in)] + ([("pool", x_upd2[2])]
                                        if qt == 3 and fc == 0 else []))
                op(lambda eng: eng.tensor_tensor(
                    tmp2, tmp2, bc[:, 512:1024], ALU.mult))
                op(lambda eng, fc=fc, qt=qt: eng.tensor_tensor(
                    x_sb[:, T * fc + 512 * qt: T * fc + 512 * qt + 512],
                    x_sb[:, T * fc + 512 * qt: T * fc + 512 * qt + 512],
                    tmp2, ALU.add))
                x_upd_fc[qt][fc] = CNT["pool"] if qt < 3 else 0
            x_upd2[qt] = CNT["pool"] if qt < 3 else 0
            ln_state["bc_reader"] = CNT["dve"]
            ln_state["bc_reader_pool"] = CNT["pool"]
            if qt == 0:
                emit_ln3_stats(0)
                emit_ln3_apply(0)

        h_war = [0] * HC
        yA2_war = [0] * FC
        h_gelu_cnt = [0] * HC

        for qt in range(NT):
            if qt == 0:
                w2_ring_war = [last_wo_mm[0], last_wo_mm[0]]

            w2_pre = [0, 0]
            for hg in range(16):
                slot = hg % 4
                if hg < 4:
                    gld = w1_pre[hg]
                else:
                    gld = es(functools.partial(w1load, hg=hg, slot=slot),
                             W1SEM[slot], [("pe", w1_ring_war[slot])])
                if hg < 2:
                    s2 = "w2a" if hg == 0 else "w2b"
                    w2_pre[hg] = es(
                        functools.partial(w2load, wg=hg, slot=hg),
                        s2, [("pe", w2_ring_war[hg]), ("c2s", 16)])
                if hg == 11 and qt + 1 < NT:
                    emit_ln3_stats(qt + 1)
                for hh2 in range(2):
                    hh = 2 * hg + hh2
                    bank = hh % 2
                    for fc in range(FC):
                        w = [("dve", xn2_fc[fc])]
                        if fc == 0:
                            w += [(W1SEM[slot], gld),
                                  ("act", pw_act[bank]),
                                  ("dve", pw_dve[bank])]
                        et(lambda eng, fc=fc, hh2=hh2, slot=slot, bank=bank:
                           eng.matmul(
                               P[bank][:, :],
                               wa[:, 2048 * slot + 256 * fc + 128 * hh2:
                                  2048 * slot + 256 * fc + 128 * hh2 + 128],
                               xn2_v[:, 512 * fc:512 * fc + 512],
                               start=(fc == 0), stop=(fc == FC - 1)), w,
                           inc=(fc == FC - 1))
                    mmh = CNT["pe"]
                    w1_ring_war[slot] = mmh
                    e = ea(lambda eng, hh=hh, bank=bank: eng.activation(
                        qh[:, 512 * hh:512 * hh + 512], P[bank][:, :],
                        AF.Gelu, bias=b1_sb[:, hh:hh + 1]),
                        [("pe", mmh), ("pe", h_war[hh]), LDC])
                    pw_act[bank] = e
                    h_gelu_cnt[hh] = e

            w1_last_mm[0] = CNT["pe"]
            if qt + 1 < NT:
                for u in range(4):
                    w1_pre[u] = es(
                        functools.partial(w1load, hg=u, slot=u),
                        W1SEM[u], [("pe", w1_ring_war[u])])
                emit_ln3_apply(qt + 1)

            first_hh = [True] * FC
            last_w2_mm = [0] * FC
            for wg in range(FC):
                slot = wg % 2
                sem = "w2a" if slot == 0 else "w2b"
                if wg < 2:
                    gld = w2_pre[wg]
                else:
                    gld = es(functools.partial(w2load, wg=wg, slot=slot),
                             sem, [("pe", w2_ring_war[slot])])
                for hh4 in range(4):
                    hh = 4 * wg + hh4
                    for ec in range(FC):
                        w = [("act", h_gelu_cnt[hh]), (sem, gld)]
                        if first_hh[ec]:
                            w += [("act", pw_act[ec]), ("dve", pw_dve[ec]),
                                  ("act", yA2_war[ec])]
                            first_hh[ec] = False
                        do_inc = (hh == HC - 1 or
                                  (hh4 == 3 and ec == FC - 1))
                        r = et(
                            lambda eng, hh=hh, hh4=hh4, ec=ec, slot=slot:
                            eng.matmul(
                                P[ec][:, :],
                                wa[:, 8192 + 4096 * slot + 1024 * hh4
                                   + 128 * ec:
                                   8192 + 4096 * slot + 1024 * hh4
                                   + 128 * ec + 128],
                                qh[:, 512 * hh:512 * hh + 512],
                                start=(hh == 0), stop=(hh == HC - 1)), w,
                            inc=do_inc)
                        if hh == HC - 1:
                            last_w2_mm[ec] = r
                w2_ring_war[slot] = CNT["pe"]
            for ec in range(FC):
                e = ea(lambda eng, ec=ec: eng.activation(
                    yA[:, 512 * ec:512 * ec + 512], P[ec][:, :],
                    AF.Identity, bias=b2_sb[:, ec:ec + 1]),
                    [("pe", last_w2_mm[ec]), ("dve", yA_war[0]), LDC])
                yA2_war[ec] = e
                pw_act[ec] = e
            for hh in range(HC):
                h_war[hh] = CNT["pe"]

            ybf4 = (lambda fc, qt=qt:
                    bfA[:, T * fc + 512 * qt: T * fc + 512 * qt + 512])
            bc_in, cast_done = ln_stats(
                lambda fc: yA[:, 512 * fc:512 * fc + 512],
                keep=ybf4, fc_wait=lambda fc: [("act", yA2_war[fc])])
            yA_war[0] = cast_done
            for fc in range(FC):
                ev(lambda eng, fc=fc, ybf4=ybf4: eng.tensor_tensor(
                    tmp[:, :], ybf4(fc),
                    bc[:, 0:512], ALU.add), [("lnb", bc_in)])
                ev(lambda eng: eng.tensor_tensor(
                    tmp[:, :], tmp[:, :], bc[:, 512:1024], ALU.mult))
                xu = ev(lambda eng, fc=fc, qt=qt: eng.tensor_tensor(
                    x_sb[:, T * fc + 512 * qt: T * fc + 512 * qt + 512],
                    x_sb[:, T * fc + 512 * qt: T * fc + 512 * qt + 512],
                    tmp[:, :], ALU.add))
                es(lambda eng, fc=fc, qt=qt: eng.dma_start(
                    out=yout[128 * fc:128 * fc + 128,
                             512 * qt:512 * qt + 512],
                    in_=x_sb[:, T * fc + 512 * qt:
                             T * fc + 512 * qt + 512]),
                    "out", [("dve", xu)])
            ln_state["bc_reader"] = CNT["dve"]

        emit("sync", lambda eng: eng.nop(), [("out", CNT["out"])])

        block = en(nc.Block())

        def runner(name):
            def run(eng):
                for fn, waits, inc, amt in SCHED[name]:
                    for s, v in waits:
                        eng.wait_ge(SEM[s], v)
                    inst = fn(eng)
                    if inc is not None:
                        inst.then_inc(SEM[inc], amt)
            return run

        block.sync(runner("sync"))
        block.gpsimd(runner("gpsimd"))
        block.tensor(runner("tensor"))
        block.vector(runner("vector"))
        block.scalar(runner("scalar"))

    return nc


def kernel(L0_attn_x, L0_attn_W_q, L0_attn_b_q, L0_attn_W_k, L0_attn_b_k,
           L0_attn_W_v, L0_attn_b_v, L0_attn_W_o, L0_attn_b_o,
           L0_ffn_w1, L0_ffn_b1, L0_ffn_w2, L0_ffn_b2):
    x = np.asarray(L0_attn_x, dtype=np.float32)
    wq_h = np.ascontiguousarray(np.asarray(L0_attn_W_q, np.float32).T)
    wk_h = np.ascontiguousarray(np.asarray(L0_attn_W_k, np.float32).T)
    wv_h = np.ascontiguousarray(np.asarray(L0_attn_W_v, np.float32).T)
    wo_h = np.ascontiguousarray(np.asarray(L0_attn_W_o, np.float32).T)
    w1_h = np.ascontiguousarray(np.asarray(L0_ffn_w1, np.float32))
    w2_h = np.ascontiguousarray(np.asarray(L0_ffn_w2, np.float32))
    bq_h = np.ascontiguousarray(
        np.asarray(L0_attn_b_q, np.float32).reshape(8, 128).T)
    bo_h = np.ascontiguousarray(
        np.asarray(L0_attn_b_o, np.float32).reshape(8, 128).T)
    b1_h = np.ascontiguousarray(
        np.asarray(L0_ffn_b1, np.float32).reshape(32, 128).T)
    b2_h = np.ascontiguousarray(
        np.asarray(L0_ffn_b2, np.float32).reshape(8, 128).T)
    bkv_h = np.ascontiguousarray(np.concatenate(
        [np.asarray(L0_attn_b_k, np.float32),
         np.asarray(L0_attn_b_v, np.float32)])[None, :])
    ident_h = np.eye(128, dtype=BF)
    ones_h = np.ones((128, 128), dtype=BF)
    eps_h = np.full((128, 1), LN_EPS, np.float32)

    nc = build_program()
    in_maps = []
    for c in range(NCORE):
        dec = np.exp(-(np.arange(T, dtype=np.float64) + T * c)
                     / float(SEQ)).astype(np.float32)
        decbc_h = np.ascontiguousarray(
            np.broadcast_to(dec[None, :], (128, T))).astype(BF)
        deccol_h = np.ascontiguousarray(dec.reshape(NCH, 128).T)
        xs = np.ascontiguousarray(x[0, T * c:T * (c + 1), :].T)
        in_maps.append({
            "xin": xs, "wq": wq_h, "wk": wk_h, "wv": wv_h, "wo": wo_h,
            "w1": w1_h, "w2": w2_h, "bq": bq_h, "bo": bo_h,
            "b1": b1_h, "b2": b2_h, "bkv": bkv_h,
            "decbc": decbc_h, "deccol": deccol_h,
            "ident": ident_h, "onesin": ones_h, "epsin": eps_h,
        })

    res = run_bass_kernel_spmd(nc, in_maps, core_ids=list(range(NCORE)))
    out = np.empty((1, SEQ, D), np.float32)
    for c in range(NCORE):
        out[0, T * c:T * (c + 1), :] = res.results[c]["y"].T
    return out


# version 15
# speedup vs baseline: 1.1292x; 1.1292x over previous
import contextlib
import functools
import numpy as np
import ml_dtypes
import concourse.bass as bass
import concourse.mybir as mybir
from concourse.bass_utils import run_bass_kernel_spmd

F32 = mybir.dt.float32
BF16 = mybir.dt.bfloat16
AF = mybir.ActivationFunctionType
ALU = mybir.AluOpType
BF = ml_dtypes.bfloat16

NCORE = 8
SEQ = 16384
D = 1024
T = SEQ // NCORE
FC = 8
NT = 4
NCH = 16
NP = 8
DH = 4096
HC = 32
LN_EPS = 1e-5
ATT_EPS = 1e-6
INVD = 1.0 / D

SEMS = ("pe", "act", "dve", "cc", "pool",
        "sx0", "sx1", "sx2", "sx3", "sc", "sbk", "swk", "swv", "swq", "swo",
        "w1a", "w1b", "w1c", "w1d", "w2a", "w2b", "g1s", "g2s",
        "lnb", "out")


def build_program():
    nc = bass.Bass("TRN2", target_bir_lowering=False)

    xin = nc.dram_tensor("xin", [D, T], F32, kind="ExternalInput")
    wq = nc.dram_tensor("wq", [D, D], F32, kind="ExternalInput")
    wk = nc.dram_tensor("wk", [D, D], F32, kind="ExternalInput")
    wv = nc.dram_tensor("wv", [D, D], F32, kind="ExternalInput")
    wo = nc.dram_tensor("wo", [D, D], F32, kind="ExternalInput")
    w1 = nc.dram_tensor("w1", [D, DH], F32, kind="ExternalInput")
    w2 = nc.dram_tensor("w2", [DH, D], F32, kind="ExternalInput")
    bq = nc.dram_tensor("bq", [128, 8], F32, kind="ExternalInput")
    bo = nc.dram_tensor("bo", [128, 8], F32, kind="ExternalInput")
    b1 = nc.dram_tensor("b1", [128, 32], F32, kind="ExternalInput")
    b2 = nc.dram_tensor("b2", [128, 8], F32, kind="ExternalInput")
    bkv = nc.dram_tensor("bkv", [1, 2 * D], F32, kind="ExternalInput")
    decbc = nc.dram_tensor("decbc", [128, T], BF16, kind="ExternalInput")
    deccol = nc.dram_tensor("deccol", [128, NCH], F32, kind="ExternalInput")
    ident = nc.dram_tensor("ident", [128, 128], BF16, kind="ExternalInput")
    onesin = nc.dram_tensor("onesin", [128, 128], BF16, kind="ExternalInput")
    epsin = nc.dram_tensor("epsin", [128, 1], F32, kind="ExternalInput")
    yout = nc.dram_tensor("y", [D, T], F32, kind="ExternalOutput")

    rowb = nc.dram_tensor("rowb", [4, 1024], F32)
    cc_in = nc.dram_tensor("cc_in", [128, 1040], BF16)
    cc_out = nc.dram_tensor("cc_out", [128, 1040], BF16)

    SCHED = {k: [] for k in ("sync", "gpsimd", "tensor", "vector", "scalar")}
    CNT = {k: 0 for k in SEMS}

    WAITED = {k: {} for k in SCHED}

    def emit(eng, fn, waits=(), inc=None, amt=1):
        wd = WAITED[eng]
        kept = []
        for sm, v in waits:
            if v > 0 and v > wd.get(sm, 0):
                kept.append((sm, v))
                wd[sm] = v
        SCHED[eng].append((fn, tuple(kept), inc, amt))
        if inc is not None:
            CNT[inc] += amt
        return CNT[inc] if inc is not None else None

    def ev(fn, waits=()):
        return emit("vector", fn, list(waits) + [("dve", CNT["dve"])], "dve")

    def ea(fn, waits=()):
        return emit("scalar", fn, list(waits) + [("act", CNT["act"])], "act")

    def et(fn, waits=(), inc=False):
        return emit("tensor", fn, waits, "pe" if inc else None)

    def es(fn, sem, waits=()):
        return emit("sync", fn, waits, sem, 16)

    def eg(fn, sem, waits=()):
        return emit("gpsimd", fn, waits, sem, 16)

    def ep(fn, waits=()):
        return emit("gpsimd", fn,
                    list(waits) + [("pool", CNT["pool"])], "pool")

    with contextlib.ExitStack() as ctx:
        en = ctx.enter_context

        x_sb = en(nc.sbuf_tensor("x_sb", [128, FC * T], F32))
        bfA = en(nc.sbuf_tensor("bfA", [128, FC * T], BF16))
        qh = en(nc.sbuf_tensor("qh", [128, FC * T], BF16))
        wa = en(nc.sbuf_tensor("wa", [128, 16384], BF16))
        tmp = en(nc.sbuf_tensor("tmp", [128, 512], F32))
        bc = en(nc.sbuf_tensor("bc", [128, 1024], F32))
        rows = en(nc.sbuf_tensor("rows", [128, 1536], F32))
        dcol = en(nc.sbuf_tensor("dcol", [128, NCH], F32))
        xbf = en(nc.sbuf_tensor("xbf", [128, 1024], BF16))
        sqb = en(nc.sbuf_tensor("sqb", [128, 1024], BF16))
        kvacc = en(nc.sbuf_tensor("kvacc", [128, 1040], F32))
        kvbd = en(nc.sbuf_tensor("kvbd", [128, 1040], BF16))
        recip = en(nc.sbuf_tensor("recip", [128, 16], F32))
        id_sb = en(nc.sbuf_tensor("id_sb", [128, 128], BF16))
        on_sb = en(nc.sbuf_tensor("on_sb", [128, 128], BF16))
        bq_sb = en(nc.sbuf_tensor("bq_sb", [128, 8], F32))
        bo_sb = en(nc.sbuf_tensor("bo_sb", [128, 8], F32))
        b1_sb = en(nc.sbuf_tensor("b1_sb", [128, 32], F32))
        b2_sb = en(nc.sbuf_tensor("b2_sb", [128, 8], F32))
        eps_sb = en(nc.sbuf_tensor("eps_sb", [128, 1], F32))
        par = en(nc.sbuf_tensor("par", [128, 2064], F32))
        yA = en(nc.sbuf_tensor("yA", [128, 4096], F32))

        parb = par.ap().bitcast(BF16)
        yab = yA.ap().bitcast(BF16)
        bkv_sb = yA.ap().bitcast(BF16)[0:1, 0:2 * D]
        decbc_v = parb[:, 0:2048]
        kfr_s = [parb[:, 2048:3072], None]
        vr_s = [parb[:, 3072:4112], None]
        xn2_v = parb[:, 0:4096]
        odv_v = kvacc.ap().bitcast(BF16)[:, 0:1024]
        kfr_s[1] = kvbd.ap()[:, 0:1024]
        vr_s[1] = yab[:, 2048:3088]
        tmp2 = kvacc.ap()[:, 512:1024]

        P = [en(nc.psum_tensor(f"P{i}", [128, 512], F32)) for i in range(8)]
        TRB = [P[6].ap().bitcast(BF16)[:, 0:128],
               P[7].ap().bitcast(BF16)[:, 0:128]]
        pw_act = [0] * 8
        pw_dve = [0] * 8

        SEM = {}
        for s in SEMS:
            SEM[s] = en(nc.semaphore(s))


        for t in range(NT):
            def f(eng, t=t):
                return eng.dma_start(
                    out=x_sb.ap().rearrange("p (c n) -> p c n", c=FC)[
                        :, :, 512 * t:512 * (t + 1)],
                    in_=xin.ap().rearrange("(c p) n -> p c n", p=128)[
                        :, :, 512 * t:512 * (t + 1)])
            es(f, "sx%d" % t)
        XL = [("sx%d" % t, 16) for t in range(NT)]

        for dst, src in ((dcol, deccol), (id_sb, ident), (on_sb, onesin),
                         (bq_sb, bq), (bo_sb, bo), (b1_sb, b1), (b2_sb, b2),
                         (eps_sb, epsin)):
            def f(eng, dst=dst, src=src):
                return eng.dma_start(out=dst[:, :], in_=src[:, :])
            es(f, "sc")

        def f(eng):
            return eng.dma_start(out=decbc_v, in_=decbc[:, :])
        es(f, "sc")
        LDC = ("sc", CNT["sc"])

        eg(lambda eng: eng.dma_start(out=bkv_sb, in_=bkv[:, :]), "sbk")

        def wload(eng, dst0, src):
            return eng.dma_start(
                out=wa.ap()[:, dst0:dst0 + 8192].rearrange(
                    "p (c o) -> p c o", c=FC),
                in_=src.ap().rearrange("(c p) o -> p c o", p=128))
        eg(functools.partial(wload, dst0=0, src=wk), "swk")
        eg(functools.partial(wload, dst0=8192, src=wv), "swv")

        def wqload(eng):
            return eng.dma_start(
                out=qh.ap()[:, 8192:16384].rearrange(
                    "p (a c o) -> p a c o", a=FC, c=FC),
                in_=wq.ap().rearrange("(c p) (a o) -> p a c o",
                                      p=128, o=128))
        eg(wqload, "swq")

        ev(lambda eng: eng.memset(
            vr_s[0].rearrange("p (s c) -> p s c", c=130)[:, :, 128:130], 1.0))
        d_ones = ev(lambda eng: eng.memset(
            vr_s[1].rearrange("p (s c) -> p s c", c=130)[:, :, 128:130], 1.0))

        ln_state = {"slot": 0, "bc_reader": 0, "bc_reader_pool": 0,
                    "row_war": 0, "stats_read": 0}
        xbf_war = [0, 0]

        def ln_stats(src_ap, extra_waits=(), keep=None, fc_wait=None,
                     bc_dst=None, split=False):
            extra = list(extra_waits)
            slot_guard = [xbf_war[0], xbf_war[1]]
            casts = [0] * FC
            dsts = [None] * FC
            for fc in range(FC):
                sl = fc % 2
                dst = keep(fc) if keep is not None else                     xbf[:, 512 * sl:512 * sl + 512]
                dsts[fc] = dst
                cast = ev(lambda eng, fc=fc, dst=dst: eng.tensor_copy(
                    dst, src_ap(fc)),
                    (extra if fc <= 1 else [])
                    + (fc_wait(fc) if fc_wait is not None else [])
                    + ([("pe", slot_guard[sl])] if keep is None else []))
                casts[fc] = cast
                if split:
                    continue
                _ln_fc_rest(fc, dst, cast, slot_guard, extra)
            cast_done = casts[FC - 1]
            if split:
                assert keep is not None
                return casts, cast_done, functools.partial(
                    _ln_rest, casts, dsts, extra, bc_dst, slot_guard)
            return (_ln_rest(None, None, extra, bc_dst, slot_guard),
                    cast_done, casts)

        _ln_sums = [0, 0]

        def _ln_fc_rest(fc, dst, cast, slot_guard, extra):
            sl = fc % 2
            sq = ea(lambda eng, dst=dst, sl=sl: eng.activation(
                sqb[:, 512 * sl:512 * sl + 512], dst, AF.Square),
                [("dve", cast), ("pe", slot_guard[sl]), LDC])
            w_mm = [("dve", cast)]
            if fc == 0:
                w_mm += extra + [("dve", ln_state["stats_read"]),
                                 ("act", pw_act[6]), ("dve", pw_dve[6]),
                                 LDC]
            et(lambda eng, dst=dst, fc=fc: eng.matmul(
                P[6][0:1, :], on_sb[:, 0:1], dst,
                start=(fc == 0), stop=(fc == FC - 1)), w_mm)
            w_mm2 = [("act", sq)]
            if fc == 0:
                w_mm2 += extra + [("dve", ln_state["stats_read"]),
                                  ("act", pw_act[7]), ("dve", pw_dve[7])]
            r = et(lambda eng, sl=sl, fc=fc: eng.matmul(
                P[7][0:1, :], on_sb[:, 0:1],
                sqb[:, 512 * sl:512 * sl + 512],
                start=(fc == 0), stop=(fc == FC - 1)), w_mm2, inc=True)
            slot_guard[sl] = r
            if fc == FC - 1:
                _ln_sums[0] = r
                _ln_sums[1] = r

        def _ln_rest(casts, dsts, extra, bc_dst, slot_guard):
            if casts is not None:
                for fc in range(FC):
                    _ln_fc_rest(fc, dsts[fc], casts[fc], slot_guard, extra)
            pe_sum, pe_sq = _ln_sums
            xbf_war[0] = slot_guard[0]
            xbf_war[1] = slot_guard[1]
            ev(lambda eng: eng.tensor_scalar(
                rows[0:1, 0:512], P[6][0:1, :], -INVD, None, ALU.mult),
                [("pe", pe_sum), ("lnb", ln_state["row_war"])])
            ev(lambda eng: eng.tensor_scalar(
                rows[0:1, 1024:1536], P[7][0:1, :], INVD, None, ALU.mult),
                [("pe", pe_sq)])
            ev(lambda eng: eng.tensor_tensor(
                rows[0:1, 512:1024], rows[0:1, 0:512], rows[0:1, 0:512],
                ALU.mult))
            d_var = ev(lambda eng: eng.tensor_tensor(
                rows[0:1, 1024:1536], rows[0:1, 1024:1536],
                rows[0:1, 512:1024], ALU.subtract))
            ln_state["stats_read"] = d_var
            pw_dve[6] = d_var
            pw_dve[7] = d_var
            ea(lambda eng: eng.activation(
                rows[0:1, 1024:1536], rows[0:1, 1024:1536], AF.Ln,
                bias=eps_sb[0:1, :]), [("dve", d_var), LDC])
            a_exp = ea(lambda eng: eng.activation(
                rows[0:1, 512:1024], rows[0:1, 1024:1536], AF.Exp,
                scale=-0.5))
            slot = ln_state["slot"] % 4
            ln_state["slot"] += 1
            b_out = es(lambda eng, slot=slot: eng.dma_start(
                out=rowb[slot:slot + 1, :], in_=rows[0:1, 0:1024]),
                "lnb", [("act", a_exp), ("lnb", ln_state["row_war"])])
            dst = bc[:, 0:1024] if bc_dst is None else bc_dst
            bc_in = es(lambda eng, slot=slot, dst=dst: eng.dma_start(
                out=dst,
                in_=rowb[slot:slot + 1, :].partition_broadcast(128)),
                "lnb", [("lnb", b_out), ("dve", ln_state["bc_reader"]),
                        ("pool", ln_state["bc_reader_pool"])])
            ln_state["row_war"] = bc_in
            return bc_in

        xn_done = [0] * NT

        def ln1_stats(t):
            return ln_stats(
                lambda fc, t=t: x_sb[:, T * fc + 512 * t:
                                     T * fc + 512 * t + 512],
                extra_waits=[XL[t]])[0]

        def ln1_apply(t, bc_in):
            for fc in range(FC):
                ev(lambda eng, fc=fc, t=t: eng.tensor_tensor(
                    tmp[:, :], x_sb[:, T * fc + 512 * t:
                                    T * fc + 512 * t + 512],
                    bc[:, 0:512], ALU.add), [("lnb", bc_in)])
                ev(lambda eng, fc=fc, t=t: eng.tensor_tensor(
                    bfA[:, T * fc + 512 * t: T * fc + 512 * t + 512],
                    tmp[:, :], bc[:, 512:1024], ALU.mult))
            xn_done[t] = CNT["dve"]
            ln_state["bc_reader"] = CNT["dve"]

        kf_war = [0, 0]
        vr_war = [0, 0]
        kv_mm_last = [0]
        last_k_mm = [0]
        last_v_mm = [0]
        kv_add_last = [0]

        bc0 = ln1_stats(0)
        ln1_apply(0, bc0)
        ln1_pend = [None]

        def kv_chunk(c):
            t = c // 4
            if c % 4 == 2 and t + 1 < NT:
                ln1_pend[0] = ln1_stats(t + 1)
            if c % 4 == 3 and ln1_pend[0] is not None:
                ln1_apply(t + 1, ln1_pend[0])
                ln1_pend[0] = None
            sl_ = c % 2
            kfr_v = kfr_s[sl_]
            vr_v = vr_s[sl_]
            for ot in range(2):
                bank = ot
                for fc in range(FC):
                    w = []
                    if fc == 0:
                        w = [("swk", 16), ("dve", xn_done[t]),
                             ("act", pw_act[bank]), ("dve", pw_dve[bank]),
                             ("sbk", 16), LDC]
                    et(lambda eng, fc=fc, ot=ot, c=c, bank=bank: eng.matmul(
                        P[bank][:, :],
                        bfA[:, T * fc + 128 * c: T * fc + 128 * c + 128],
                        wa[:, 1024 * fc + 512 * ot:
                           1024 * fc + 512 * ot + 512],
                        start=(fc == 0), stop=False), w)
                mmk = et(lambda eng, ot=ot, bank=bank: eng.matmul(
                    P[bank][:, :], on_sb[0:1, 0:128],
                    bkv_sb[0:1, 512 * ot:512 * ot + 512],
                    start=False, stop=True), [], inc=True)
                last_k_mm[0] = mmk
                e = ea(lambda eng, ot=ot, c=c, bank=bank, kfr_v=kfr_v:
                       eng.activation(
                    kfr_v[:, 512 * ot:512 * ot + 512], P[bank][:, :],
                    AF.Relu, scale=dcol[:, c:c + 1]),
                    [("pe", mmk), ("pe", kf_war[sl_]), LDC])
                pw_act[bank] = e
            for ot in range(2):
                bank = 2 + ot
                for fc in range(FC):
                    w = []
                    if fc == 0:
                        w = [("swv", 16), ("dve", xn_done[t]),
                             ("act", pw_act[bank]), ("dve", pw_dve[bank])]
                    et(lambda eng, fc=fc, ot=ot, c=c, bank=bank: eng.matmul(
                        P[bank][:, :],
                        bfA[:, T * fc + 128 * c: T * fc + 128 * c + 128],
                        wa[:, 8192 + 1024 * fc + 512 * ot:
                           8192 + 1024 * fc + 512 * ot + 512],
                        start=(fc == 0), stop=False), w)
                mmv = et(lambda eng, ot=ot, bank=bank: eng.matmul(
                    P[bank][:, :], on_sb[0:1, 0:128],
                    bkv_sb[0:1, D + 512 * ot:D + 512 * ot + 512],
                    start=False, stop=True), [], inc=True)
                last_v_mm[0] = mmv
                e = ea(lambda eng, ot=ot, bank=bank, vr_v=vr_v:
                       eng.activation(
                    vr_v.rearrange("p (s c) -> p s c", c=130)[
                        :, 4 * ot:4 * ot + 4, 0:128],
                    P[bank][:, 0:512].rearrange("p (s c) -> p s c", c=128),
                    AF.Copy),
                    [("pe", mmv), ("pe", vr_war[sl_])])
                pw_act[bank] = e
            kevac_q[c] = CNT["act"]
            if c > 0:
                kv_pairs(c - 1)

        def kv_pairs(c):
            sl_ = c % 2
            kfr_v = kfr_s[sl_]
            vr_v = vr_s[sl_]
            kevac = kevac_q[c]
            for p in range(NP):
                bank = 4 + (p % 2)
                w = [("act", kevac), ("dve", pw_dve[bank])]
                if c <= 1 and p < 2:
                    w.append(("dve", d_ones))
                mm = et(lambda eng, p=p, bank=bank, kfr_v=kfr_v, vr_v=vr_v:
                        eng.matmul(
                    P[bank][:, 0:130],
                    kfr_v[:, 128 * p:128 * p + 128],
                    vr_v[:, 130 * p:130 * p + 130],
                    start=True, stop=True), w, inc=True)
                if c == 0:
                    a = ev(lambda eng, p=p, bank=bank: eng.tensor_copy(
                        kvacc[:, 130 * p:130 * p + 130], P[bank][:, 0:130]),
                        [("pe", mm)])
                else:
                    a = ev(lambda eng, p=p, bank=bank: eng.tensor_tensor(
                        kvacc[:, 130 * p:130 * p + 130],
                        kvacc[:, 130 * p:130 * p + 130],
                        P[bank][:, 0:130], ALU.add), [("pe", mm)])
                pw_dve[bank] = a
                kv_mm_last[0] = mm
            kf_war[sl_] = kv_mm_last[0]
            vr_war[sl_] = kv_mm_last[0]
            kv_add_last[0] = CNT["dve"]

        kevac_q = [0] * NCH
        for c in range(NCH):
            kv_chunk(c)
        kv_pairs(NCH - 1)

        cc_cast = ev(lambda eng: eng.tensor_copy(kvbd[:, :], kvacc[:, :]),
                     [("dve", kv_add_last[0]), ("pe", kv_mm_last[0])])
        eg(lambda eng: eng.dma_start(out=cc_in[:, :], in_=kvbd[:, :]),
           "g1s", [("dve", cc_cast)])

        def coll(eng):
            return eng.collective_compute(
                "AllReduce", ALU.add,
                replica_groups=[list(range(NCORE))],
                ins=[cc_in.ap().opt()],
                outs=[cc_out.ap().opt()])
        emit("gpsimd", coll, [("g1s", 16)], "cc", 1)
        eg(lambda eng: eng.dma_start(out=kvbd[:, :], in_=cc_out[:, :]),
           "g2s", [("cc", 1)])
        eg(functools.partial(wload, dst0=8192, src=wo), "swo",
           [("pe", last_v_mm[0])])

        tmp_war = [CNT["dve"]]
        last_q_mm = [0]
        for oc in range(FC):
            mmq_t = [0] * NT
            for t in range(NT):
                bank = 4 * (oc % 2) + t
                for fc in range(FC):
                    w = []
                    if fc == 0:
                        w = [("swq", 16),
                             ("act", pw_act[bank]), ("dve", pw_dve[bank])]
                    et(lambda eng, fc=fc, oc=oc, t=t, bank=bank: eng.matmul(
                        P[bank][:, :],
                        qh[:, 8192 + 1024 * oc + 128 * fc:
                           8192 + 1024 * oc + 128 * fc + 128],
                        bfA[:, T * fc + 512 * t: T * fc + 512 * t + 512],
                        start=(fc == 0), stop=(fc == FC - 1)), w,
                        inc=(fc == FC - 1))
                mmq_t[t] = CNT["pe"]
            last_q_mm[0] = mmq_t[NT - 1]
            for t in range(NT):
                bank = 4 * (oc % 2) + t
                e = ea(lambda eng, oc=oc, bank=bank: eng.activation(
                    tmp[:, :], P[bank][:, :], AF.Relu,
                    bias=bq_sb[:, oc:oc + 1]),
                    [("pe", mmq_t[NT - 1]), ("dve", tmp_war[0]), LDC])
                m = ev(lambda eng, oc=oc, t=t: eng.tensor_tensor(
                    qh[:, T * oc + 512 * t: T * oc + 512 * t + 512],
                    tmp[:, :], decbc_v[:, 512 * t:512 * t + 512], ALU.mult),
                    [("act", e)])
                tmp_war[0] = m
                pw_act[bank] = e
        qf_done = CNT["dve"]

        kbd3 = kvbd.ap().rearrange("p (s c) -> p s c", c=130)
        ev(lambda eng: eng.memset(kbd3[0:64, :, 64:128], 0.0), [("g2s", 16)])
        ev(lambda eng: eng.memset(kbd3[64:128, :, 0:64], 0.0))
        ev(lambda eng: eng.memset(kbd3[64:128, :, 128:129], 0.0))
        kvbd_done = ev(lambda eng: eng.memset(kbd3[0:64, :, 129:130], 0.0))

        odv_war = [0] * NP
        trev = [[0] * NCH for _ in range(NP)]

        def out_chunk(c, wo_qtm1=None):
            for half in range(2):
                mm_out = [0] * 4
                for j in range(4):
                    p = 4 * half + j
                    bank = j
                    w = [("dve", kvbd_done), ("act", pw_act[bank]),
                         ("dve", pw_dve[bank])]
                    mm_out[j] = et(lambda eng, c=c, p=p, bank=bank:
                                   eng.matmul(
                        P[bank][:, 0:130],
                        qh[:, T * p + 128 * c: T * p + 128 * c + 128],
                        kvbd[:, 130 * p:130 * p + 130],
                        start=True, stop=True), w, inc=True)
                for j in range(4):
                    p = 4 * half + j
                    ev(lambda eng, p=p, bank=j: eng.tensor_scalar(
                        recip.ap()[:, 2 * p:2 * p + 2],
                        P[bank][:, 128:130],
                        ATT_EPS, None, ALU.add), [("pe", mm_out[j])])
                ev(lambda eng, half=half: eng.reciprocal(
                    recip[:, 8 * half:8 * half + 8],
                    recip[:, 8 * half:8 * half + 8]))
                if wo_qtm1 is not None:
                    wo_group(wo_qtm1, 2 * (c % 4) + half)
                for j in range(4):
                    p = 4 * half + j
                    bank = j
                    dvm = ev(lambda eng, p=p, bank=bank: eng.tensor_tensor(
                        odv_v.rearrange("p (s c) -> p s c", c=64)[
                            :, 2 * p:2 * p + 2, :],
                        P[bank][:, 0:128].rearrange(
                            "p (s c) -> p s c", c=64),
                        recip.ap()[:, 2 * p:2 * p + 2].unsqueeze(2)
                        .broadcast_to([128, 2, 64]),
                        ALU.mult), [("pe", odv_war[p])])
                    pw_dve[bank] = dvm
                    tb = p % 2
                    trmm = et(lambda eng, p=p, tb=tb: eng.transpose(
                        TRB[tb], odv_v[:, 128 * p:128 * p + 128],
                        id_sb[:, :]),
                        [("dve", dvm), ("act", pw_act[6 + tb]),
                         ("dve", pw_dve[6 + tb]), LDC], inc=True)
                    odv_war[p] = trmm
                    tev = ea(lambda eng, p=p, c=c, tb=tb: eng.copy(
                        bfA[:, T * p + 128 * c: T * p + 128 * c + 128],
                        TRB[tb]), [("pe", trmm)])
                    pw_act[6 + tb] = tev
                    trev[p][c] = tev

        def w1load(eng, hg, slot):
            return eng.dma_start(
                out=wa.ap()[:, 2048 * slot:2048 * slot + 2048].rearrange(
                    "p (c h) -> p c h", c=FC),
                in_=w1.ap().rearrange("(c p) h -> p c h", p=128)[
                    :, :, 256 * hg:256 * hg + 256])

        def w2load(eng, wg, slot):
            return eng.dma_start(
                out=wa.ap()[:, 8192 + 4096 * slot:
                            8192 + 4096 * slot + 4096]
                .rearrange("p (c e) -> p c e", c=4),
                in_=w2.ap().rearrange("(c p) e -> p c e", p=128)[
                    :, 4 * wg:4 * wg + 4, :])

        w1_ring_war = [last_k_mm[0]] * 4
        w2_ring_war = [0, 0]
        W1SEM = ("w1a", "w1b", "w1c", "w1d")
        w1_pre = [0] * 4
        for u in range(4):
            w1_pre[u] = eg(functools.partial(w1load, hg=u, slot=u),
                           W1SEM[u], [("pe", last_k_mm[0])])

        xn2_ready_q = [0] * NT
        w1_last_mm = [0]
        ln3_bc = [0]
        x_upd2 = [0] * NT

        def emit_ln3_stats(qt):
            ln3_extra = []
            if qt == 0:
                ln3_extra += [("pe", kv_mm_last[0]), ("dve", qf_done)]
            ln3_bc[0] = ln_stats(
                lambda fc, qt=qt: x_sb[:, T * fc + 512 * qt:
                                       T * fc + 512 * qt + 512],
                extra_waits=ln3_extra,
                fc_wait=lambda fc, qt=qt: [("pool", x_upd_fc[qt][fc])],
                bc_dst=(rows[:, 0:1024] if qt == 0 else None))[0]

        xn2_fc = [0] * FC

        def emit_ln3_apply(qt, fcs=tuple(range(FC))):
            srcM = rows[:, 0:512] if qt == 0 else bc[:, 0:512]
            srcA = rows[:, 512:1024] if qt == 0 else bc[:, 512:1024]
            for fc in fcs:
                w = [("lnb", ln3_bc[0])]
                if fc == 0 and qt > 0:
                    w.append(("pe", w1_last_mm[0]))
                ev(lambda eng, fc=fc, qt=qt, srcM=srcM: eng.tensor_tensor(
                    tmp[:, :], x_sb[:, T * fc + 512 * qt:
                                    T * fc + 512 * qt + 512],
                    srcM, ALU.add), w)
                xn2_fc[fc] = ev(lambda eng, fc=fc, srcA=srcA:
                                eng.tensor_tensor(
                    xn2_v[:, 512 * fc:512 * fc + 512],
                    tmp[:, :], srcA, ALU.mult))
            xn2_ready_q[qt] = xn2_fc[FC - 1]
            ln_state["bc_reader"] = CNT["dve"]

        yA_war = [0]
        ya_rel = [[0] * FC]
        x_upd_fc = [[0] * FC for _ in range(NT)]
        last_wo_mm = [0]
        wo_yev = [[0] * FC for _ in range(NT)]
        wo_mm_q = [0] * NT

        def wo_group(qtm1, oc):
            bank = 4 + (oc % 2)
            for ec in range(FC):
                w = [("act", trev[ec][4 * qtm1 + 3])]
                if ec == 0:
                    w += [("swo", 16), ("act", pw_act[bank]),
                          ("dve", pw_dve[bank])]
                et(lambda eng, oc=oc, ec=ec, qtm1=qtm1, bank=bank:
                   eng.matmul(
                       P[bank][:, :],
                       wa[:, 8192 + 1024 * ec + 128 * oc:
                          8192 + 1024 * ec + 128 * oc + 128],
                       bfA[:, T * ec + 512 * qtm1:
                           T * ec + 512 * qtm1 + 512],
                       start=(ec == 0), stop=(ec == FC - 1)), w,
                   inc=(ec == FC - 1))
            mmo = CNT["pe"]
            last_wo_mm[0] = mmo
            wo_mm_q[qtm1] = mmo
            e = ea(lambda eng, oc=oc, bank=bank: eng.activation(
                yA[:, 512 * oc:512 * oc + 512], P[bank][:, :],
                AF.Identity, bias=bo_sb[:, oc:oc + 1]),
                [("pe", mmo), ("dve", ya_rel[0][oc]), LDC])
            pw_act[bank] = e
            wo_yev[qtm1][oc] = e

        ln2_pend = [None]

        def ln2_applies(qt, ybf, bc_in):
            op = ep if qt < 2 else ev
            for fc in range(FC):
                op(lambda eng, fc=fc, ybf=ybf: eng.tensor_tensor(
                    tmp2, ybf(fc), bc[:, 0:512], ALU.add),
                    [("lnb", bc_in)] + ([("pool", x_upd2[1])]
                                        if qt == 2 and fc == 0 else []))
                op(lambda eng: eng.tensor_tensor(
                    tmp2, tmp2, bc[:, 512:1024], ALU.mult))
                op(lambda eng, fc=fc, qt=qt: eng.tensor_tensor(
                    x_sb[:, T * fc + 512 * qt: T * fc + 512 * qt + 512],
                    x_sb[:, T * fc + 512 * qt: T * fc + 512 * qt + 512],
                    tmp2, ALU.add))
                x_upd_fc[qt][fc] = CNT["pool"] if qt < 2 else 0
            x_upd2[qt] = CNT["pool"] if qt < 2 else 0
            ln_state["bc_reader"] = CNT["dve"]
            ln_state["bc_reader_pool"] = CNT["pool"]

        def emit_ln2(qt, split=False):
            ybf = (lambda fc, qt=qt:
                   bfA[:, T * fc + 512 * qt: T * fc + 512 * qt + 512])
            r = ln_stats(
                lambda fc: yA[:, 512 * fc:512 * fc + 512],
                keep=ybf, extra_waits=[("pe", wo_mm_q[qt])],
                fc_wait=lambda fc, qt=qt: [("act", wo_yev[qt][fc])],
                split=split)
            if split:
                casts, cast_done, rest = r
                yA_war[0] = cast_done
                ya_rel[0] = casts
                ln2_pend[0] = (qt, ybf, rest)
                return
            bc_in, cast_done, casts = r
            yA_war[0] = cast_done
            ya_rel[0] = casts
            ln2_applies(qt, ybf, bc_in)

        def finish_ln2():
            qt, ybf, rest = ln2_pend[0]
            ln2_pend[0] = None
            ln2_applies(qt, ybf, rest())

        for qt in range(NT):
            for c in range(4 * qt, 4 * qt + 4):
                out_chunk(c, wo_qtm1=(qt - 1 if qt >= 1 else None))
                if qt == 3 and c == 13:
                    emit_ln3_apply(0)
                elif qt == 3 and c == 14:
                    finish_ln2()
            if qt == 1:
                emit_ln2(0)
            elif qt == 2:
                emit_ln3_stats(0)
                emit_ln2(1, split=True)
            elif qt == 3:
                emit_ln2(2)
        for oc in range(FC):
            wo_group(3, oc)
        emit_ln2(3)

        h_war = [0] * HC
        yA2_war = [0] * FC
        h_gelu_cnt = [0] * HC

        for qt in range(NT):
            if qt == 0:
                w2_ring_war = [last_wo_mm[0], last_wo_mm[0]]

            w2_pre = [0, 0]
            for hg in range(16):
                slot = hg % 4
                if hg < 4:
                    gld = w1_pre[hg]
                else:
                    gld = eg(functools.partial(w1load, hg=hg, slot=slot),
                             W1SEM[slot], [("pe", w1_ring_war[slot])])
                if hg < 2:
                    s2 = "w2a" if hg == 0 else "w2b"
                    w2_pre[hg] = eg(
                        functools.partial(w2load, wg=hg, slot=hg),
                        s2, [("pe", w2_ring_war[hg])])
                if hg == 11 and qt + 1 < NT:
                    emit_ln3_stats(qt + 1)
                for hh2 in range(2):
                    hh = 2 * hg + hh2
                    bank = hh % 2
                    for fc in range(FC):
                        w = [("dve", xn2_fc[fc])]
                        if fc == 0:
                            w += [(W1SEM[slot], gld),
                                  ("act", pw_act[bank]),
                                  ("dve", pw_dve[bank])]
                        et(lambda eng, fc=fc, hh2=hh2, slot=slot, bank=bank:
                           eng.matmul(
                               P[bank][:, :],
                               wa[:, 2048 * slot + 256 * fc + 128 * hh2:
                                  2048 * slot + 256 * fc + 128 * hh2 + 128],
                               xn2_v[:, 512 * fc:512 * fc + 512],
                               start=(fc == 0), stop=(fc == FC - 1)), w,
                           inc=(fc == FC - 1))
                    mmh = CNT["pe"]
                    w1_ring_war[slot] = mmh
                    e = ea(lambda eng, hh=hh, bank=bank: eng.activation(
                        qh[:, 512 * hh:512 * hh + 512], P[bank][:, :],
                        AF.Gelu, bias=b1_sb[:, hh:hh + 1]),
                        [("pe", mmh), ("pe", h_war[hh]), LDC])
                    pw_act[bank] = e
                    h_gelu_cnt[hh] = e

            w1_last_mm[0] = CNT["pe"]
            if qt + 1 < NT:
                for u in range(4):
                    w1_pre[u] = eg(
                        functools.partial(w1load, hg=u, slot=u),
                        W1SEM[u], [("pe", w1_ring_war[u])])
                emit_ln3_apply(qt + 1)

            first_hh = [True] * FC
            last_w2_mm = [0] * FC
            for wg in range(FC):
                slot = wg % 2
                sem = "w2a" if slot == 0 else "w2b"
                if wg < 2:
                    gld = w2_pre[wg]
                else:
                    gld = eg(functools.partial(w2load, wg=wg, slot=slot),
                             sem, [("pe", w2_ring_war[slot])])
                for hh4 in range(4):
                    hh = 4 * wg + hh4
                    for ec in range(FC):
                        w = [("act", h_gelu_cnt[hh]), (sem, gld)]
                        if first_hh[ec]:
                            w += [("act", pw_act[ec]), ("dve", pw_dve[ec]),
                                  ("act", yA2_war[ec])]
                            first_hh[ec] = False
                        do_inc = (hh == HC - 1 or
                                  (hh4 == 3 and ec == FC - 1))
                        r = et(
                            lambda eng, hh=hh, hh4=hh4, ec=ec, slot=slot:
                            eng.matmul(
                                P[ec][:, :],
                                wa[:, 8192 + 4096 * slot + 1024 * hh4
                                   + 128 * ec:
                                   8192 + 4096 * slot + 1024 * hh4
                                   + 128 * ec + 128],
                                qh[:, 512 * hh:512 * hh + 512],
                                start=(hh == 0), stop=(hh == HC - 1)), w,
                            inc=do_inc)
                        if hh == HC - 1:
                            last_w2_mm[ec] = r
                w2_ring_war[slot] = CNT["pe"]
            for ec in range(FC):
                e = ea(lambda eng, ec=ec: eng.activation(
                    yA[:, 512 * ec:512 * ec + 512], P[ec][:, :],
                    AF.Identity, bias=b2_sb[:, ec:ec + 1]),
                    [("pe", last_w2_mm[ec]), ("dve", yA_war[0]), LDC])
                yA2_war[ec] = e
                pw_act[ec] = e
            for hh in range(HC):
                h_war[hh] = CNT["pe"]

            ybf4 = (lambda fc, qt=qt:
                    bfA[:, T * fc + 512 * qt: T * fc + 512 * qt + 512])
            bc_in, cast_done, _ = ln_stats(
                lambda fc: yA[:, 512 * fc:512 * fc + 512],
                keep=ybf4, fc_wait=lambda fc: [("act", yA2_war[fc])])
            yA_war[0] = cast_done
            for fc in range(FC):
                ev(lambda eng, fc=fc, ybf4=ybf4: eng.tensor_tensor(
                    tmp[:, :], ybf4(fc),
                    bc[:, 0:512], ALU.add), [("lnb", bc_in)])
                ev(lambda eng: eng.tensor_tensor(
                    tmp[:, :], tmp[:, :], bc[:, 512:1024], ALU.mult))
                xu = ev(lambda eng, fc=fc, qt=qt: eng.tensor_tensor(
                    x_sb[:, T * fc + 512 * qt: T * fc + 512 * qt + 512],
                    x_sb[:, T * fc + 512 * qt: T * fc + 512 * qt + 512],
                    tmp[:, :], ALU.add))
                es(lambda eng, fc=fc, qt=qt: eng.dma_start(
                    out=yout[128 * fc:128 * fc + 128,
                             512 * qt:512 * qt + 512],
                    in_=x_sb[:, T * fc + 512 * qt:
                             T * fc + 512 * qt + 512]),
                    "out", [("dve", xu)])
            ln_state["bc_reader"] = CNT["dve"]

        emit("sync", lambda eng: eng.nop(), [("out", CNT["out"])])

        block = en(nc.Block())

        def runner(name):
            def run(eng):
                for fn, waits, inc, amt in SCHED[name]:
                    for s, v in waits:
                        eng.wait_ge(SEM[s], v)
                    inst = fn(eng)
                    if inc is not None:
                        inst.then_inc(SEM[inc], amt)
            return run

        block.sync(runner("sync"))
        block.gpsimd(runner("gpsimd"))
        block.tensor(runner("tensor"))
        block.vector(runner("vector"))
        block.scalar(runner("scalar"))

    return nc


def kernel(L0_attn_x, L0_attn_W_q, L0_attn_b_q, L0_attn_W_k, L0_attn_b_k,
           L0_attn_W_v, L0_attn_b_v, L0_attn_W_o, L0_attn_b_o,
           L0_ffn_w1, L0_ffn_b1, L0_ffn_w2, L0_ffn_b2):
    x = np.asarray(L0_attn_x, dtype=np.float32)
    wq_h = np.ascontiguousarray(np.asarray(L0_attn_W_q, np.float32).T)
    wk_h = np.ascontiguousarray(np.asarray(L0_attn_W_k, np.float32).T)
    wv_h = np.ascontiguousarray(np.asarray(L0_attn_W_v, np.float32).T)
    wo_h = np.ascontiguousarray(np.asarray(L0_attn_W_o, np.float32).T)
    w1_h = np.ascontiguousarray(np.asarray(L0_ffn_w1, np.float32))
    w2_h = np.ascontiguousarray(np.asarray(L0_ffn_w2, np.float32))
    bq_h = np.ascontiguousarray(
        np.asarray(L0_attn_b_q, np.float32).reshape(8, 128).T)
    bo_h = np.ascontiguousarray(
        np.asarray(L0_attn_b_o, np.float32).reshape(8, 128).T)
    b1_h = np.ascontiguousarray(
        np.asarray(L0_ffn_b1, np.float32).reshape(32, 128).T)
    b2_h = np.ascontiguousarray(
        np.asarray(L0_ffn_b2, np.float32).reshape(8, 128).T)
    bkv_h = np.ascontiguousarray(np.concatenate(
        [np.asarray(L0_attn_b_k, np.float32),
         np.asarray(L0_attn_b_v, np.float32)])[None, :])
    ident_h = np.eye(128, dtype=BF)
    ones_h = np.ones((128, 128), dtype=BF)
    eps_h = np.full((128, 1), LN_EPS, np.float32)

    nc = build_program()
    in_maps = []
    for c in range(NCORE):
        dec = np.exp(-(np.arange(T, dtype=np.float64) + T * c)
                     / float(SEQ)).astype(np.float32)
        decbc_h = np.ascontiguousarray(
            np.broadcast_to(dec[None, :], (128, T))).astype(BF)
        deccol_h = np.ascontiguousarray(dec.reshape(NCH, 128).T)
        xs = np.ascontiguousarray(x[0, T * c:T * (c + 1), :].T)
        in_maps.append({
            "xin": xs, "wq": wq_h, "wk": wk_h, "wv": wv_h, "wo": wo_h,
            "w1": w1_h, "w2": w2_h, "bq": bq_h, "bo": bo_h,
            "b1": b1_h, "b2": b2_h, "bkv": bkv_h,
            "decbc": decbc_h, "deccol": deccol_h,
            "ident": ident_h, "onesin": ones_h, "epsin": eps_h,
        })

    res = run_bass_kernel_spmd(nc, in_maps, core_ids=list(range(NCORE)))
    out = np.empty((1, SEQ, D), np.float32)
    for c in range(NCORE):
        out[0, T * c:T * (c + 1), :] = res.results[c]["y"].T
    return out


# version 16 (final)
# speedup vs baseline: 1.1398x; 1.0093x over previous
import contextlib
import functools
import numpy as np
import ml_dtypes
import concourse.bass as bass
import concourse.mybir as mybir
from concourse.bass_utils import run_bass_kernel_spmd

F32 = mybir.dt.float32
BF16 = mybir.dt.bfloat16
AF = mybir.ActivationFunctionType
ALU = mybir.AluOpType
BF = ml_dtypes.bfloat16

NCORE = 8
SEQ = 16384
D = 1024
T = SEQ // NCORE
FC = 8
NT = 4
NCH = 16
NP = 8
DH = 4096
HC = 32
LN_EPS = 1e-5
ATT_EPS = 1e-6
INVD = 1.0 / D

SEMS = ("pe", "act", "dve", "cc", "pool",
        "sx0", "sx1", "sx2", "sx3", "sc", "sbk", "swk", "swv", "swq", "swo",
        "w1a", "w1b", "w1c", "w1d", "w2a", "w2b", "g1s", "g2s",
        "lnb", "out")


def build_program():
    nc = bass.Bass("TRN2", target_bir_lowering=False)

    xin = nc.dram_tensor("xin", [D, T], F32, kind="ExternalInput")
    wq = nc.dram_tensor("wq", [D, D], F32, kind="ExternalInput")
    wk = nc.dram_tensor("wk", [D, D], F32, kind="ExternalInput")
    wv = nc.dram_tensor("wv", [D, D], F32, kind="ExternalInput")
    wo = nc.dram_tensor("wo", [D, D], F32, kind="ExternalInput")
    w1 = nc.dram_tensor("w1", [D, DH], F32, kind="ExternalInput")
    w2 = nc.dram_tensor("w2", [DH, D], F32, kind="ExternalInput")
    bq = nc.dram_tensor("bq", [128, 8], F32, kind="ExternalInput")
    bo = nc.dram_tensor("bo", [128, 8], F32, kind="ExternalInput")
    b1 = nc.dram_tensor("b1", [128, 32], F32, kind="ExternalInput")
    b2 = nc.dram_tensor("b2", [128, 8], F32, kind="ExternalInput")
    bkv = nc.dram_tensor("bkv", [1, 2 * D], F32, kind="ExternalInput")
    decbc = nc.dram_tensor("decbc", [128, T], BF16, kind="ExternalInput")
    deccol = nc.dram_tensor("deccol", [128, NCH], F32, kind="ExternalInput")
    ident = nc.dram_tensor("ident", [128, 128], BF16, kind="ExternalInput")
    onesin = nc.dram_tensor("onesin", [128, 128], BF16, kind="ExternalInput")
    epsin = nc.dram_tensor("epsin", [128, 1], F32, kind="ExternalInput")
    yout = nc.dram_tensor("y", [D, T], F32, kind="ExternalOutput")

    rowb = nc.dram_tensor("rowb", [4, 1024], F32)
    cc_in = nc.dram_tensor("cc_in", [128, 1040], BF16)
    cc_out = nc.dram_tensor("cc_out", [128, 1040], BF16)

    SCHED = {k: [] for k in ("sync", "gpsimd", "tensor", "vector", "scalar")}
    CNT = {k: 0 for k in SEMS}

    WAITED = {k: {} for k in SCHED}

    def emit(eng, fn, waits=(), inc=None, amt=1):
        wd = WAITED[eng]
        kept = []
        for sm, v in waits:
            if v > 0 and v > wd.get(sm, 0):
                kept.append((sm, v))
                wd[sm] = v
        SCHED[eng].append((fn, tuple(kept), inc, amt))
        if inc is not None:
            CNT[inc] += amt
        return CNT[inc] if inc is not None else None

    def ev(fn, waits=()):
        return emit("vector", fn, list(waits) + [("dve", CNT["dve"])], "dve")

    def ea(fn, waits=()):
        return emit("scalar", fn, list(waits) + [("act", CNT["act"])], "act")

    def et(fn, waits=(), inc=False):
        return emit("tensor", fn, waits, "pe" if inc else None)

    def es(fn, sem, waits=()):
        return emit("sync", fn, waits, sem, 16)

    def eg(fn, sem, waits=()):
        return emit("gpsimd", fn, waits, sem, 16)

    def ep(fn, waits=()):
        return emit("gpsimd", fn,
                    list(waits) + [("pool", CNT["pool"])], "pool")

    with contextlib.ExitStack() as ctx:
        en = ctx.enter_context

        x_sb = en(nc.sbuf_tensor("x_sb", [128, FC * T], F32))
        bfA = en(nc.sbuf_tensor("bfA", [128, FC * T], BF16))
        qh = en(nc.sbuf_tensor("qh", [128, FC * T], BF16))
        wa = en(nc.sbuf_tensor("wa", [128, 16384], BF16))
        tmp = en(nc.sbuf_tensor("tmp", [128, 512], F32))
        bc = en(nc.sbuf_tensor("bc", [128, 1024], F32))
        rows = en(nc.sbuf_tensor("rows", [128, 1536], F32))
        dcol = en(nc.sbuf_tensor("dcol", [128, NCH], F32))
        xbf = en(nc.sbuf_tensor("xbf", [128, 1024], BF16))
        sqb = en(nc.sbuf_tensor("sqb", [128, 1024], BF16))
        kvacc = en(nc.sbuf_tensor("kvacc", [128, 1040], F32))
        kvbd = en(nc.sbuf_tensor("kvbd", [128, 1040], BF16))
        recip = en(nc.sbuf_tensor("recip", [128, 16], F32))
        id_sb = en(nc.sbuf_tensor("id_sb", [128, 128], BF16))
        on_sb = en(nc.sbuf_tensor("on_sb", [128, 128], BF16))
        bq_sb = en(nc.sbuf_tensor("bq_sb", [128, 8], F32))
        bo_sb = en(nc.sbuf_tensor("bo_sb", [128, 8], F32))
        b1_sb = en(nc.sbuf_tensor("b1_sb", [128, 32], F32))
        b2_sb = en(nc.sbuf_tensor("b2_sb", [128, 8], F32))
        eps_sb = en(nc.sbuf_tensor("eps_sb", [128, 1], F32))
        par = en(nc.sbuf_tensor("par", [128, 2064], F32))
        yA = en(nc.sbuf_tensor("yA", [128, 4096], F32))

        parb = par.ap().bitcast(BF16)
        yab = yA.ap().bitcast(BF16)
        bkv_sb = yA.ap().bitcast(BF16)[0:1, 0:2 * D]
        decbc_v = parb[:, 0:2048]
        kfr_s = [parb[:, 2048:3072], None]
        vr_s = [parb[:, 3072:4112], None]
        xn2_v = parb[:, 0:4096]
        odv_v = kvacc.ap().bitcast(BF16)[:, 0:1024]
        kfr_s[1] = kvbd.ap()[:, 0:1024]
        vr_s[1] = yab[:, 2048:3088]
        tmp2 = kvacc.ap()[:, 512:1024]

        P = [en(nc.psum_tensor(f"P{i}", [128, 512], F32)) for i in range(8)]
        TRB = [P[6].ap().bitcast(BF16)[:, 0:128],
               P[7].ap().bitcast(BF16)[:, 0:128]]
        pw_act = [0] * 8
        pw_dve = [0] * 8

        SEM = {}
        for s in SEMS:
            SEM[s] = en(nc.semaphore(s))


        for t in range(NT):
            def f(eng, t=t):
                return eng.dma_start(
                    out=x_sb.ap().rearrange("p (c n) -> p c n", c=FC)[
                        :, :, 512 * t:512 * (t + 1)],
                    in_=xin.ap().rearrange("(c p) n -> p c n", p=128)[
                        :, :, 512 * t:512 * (t + 1)])
            es(f, "sx%d" % t)
        XL = [("sx%d" % t, 16) for t in range(NT)]

        for dst, src in ((dcol, deccol), (id_sb, ident), (on_sb, onesin),
                         (bq_sb, bq), (bo_sb, bo), (b1_sb, b1), (b2_sb, b2),
                         (eps_sb, epsin)):
            def f(eng, dst=dst, src=src):
                return eng.dma_start(out=dst[:, :], in_=src[:, :])
            es(f, "sc")

        def f(eng):
            return eng.dma_start(out=decbc_v, in_=decbc[:, :])
        es(f, "sc")
        LDC = ("sc", CNT["sc"])

        eg(lambda eng: eng.dma_start(out=bkv_sb, in_=bkv[:, :]), "sbk")

        def wload(eng, dst0, src):
            return eng.dma_start(
                out=wa.ap()[:, dst0:dst0 + 8192].rearrange(
                    "p (c o) -> p c o", c=FC),
                in_=src.ap().rearrange("(c p) o -> p c o", p=128))
        eg(functools.partial(wload, dst0=0, src=wk), "swk")
        eg(functools.partial(wload, dst0=8192, src=wv), "swv")

        def wqload(eng):
            return eng.dma_start(
                out=qh.ap()[:, 8192:16384].rearrange(
                    "p (a c o) -> p a c o", a=FC, c=FC),
                in_=wq.ap().rearrange("(c p) (a o) -> p a c o",
                                      p=128, o=128))
        eg(wqload, "swq")

        ev(lambda eng: eng.memset(
            vr_s[0].rearrange("p (s c) -> p s c", c=130)[:, :, 128:130], 1.0))
        d_ones = ev(lambda eng: eng.memset(
            vr_s[1].rearrange("p (s c) -> p s c", c=130)[:, :, 128:130], 1.0))

        ln_state = {"slot": 0, "bc_reader": 0, "bc_reader_pool": 0,
                    "row_war": 0, "stats_read": 0}
        xbf_war = [0, 0]

        def ln_stats(src_ap, extra_waits=(), keep=None, fc_wait=None,
                     bc_dst=None, split=False):
            extra = list(extra_waits)
            slot_guard = [xbf_war[0], xbf_war[1]]
            casts = [0] * FC
            dsts = [None] * FC
            for fc in range(FC):
                sl = fc % 2
                dst = keep(fc) if keep is not None else                     xbf[:, 512 * sl:512 * sl + 512]
                dsts[fc] = dst
                cast = ev(lambda eng, fc=fc, dst=dst: eng.tensor_copy(
                    dst, src_ap(fc)),
                    (extra if fc <= 1 else [])
                    + (fc_wait(fc) if fc_wait is not None else [])
                    + ([("pe", slot_guard[sl])] if keep is None else []))
                casts[fc] = cast
                if split:
                    continue
                _ln_fc_rest(fc, dst, cast, slot_guard, extra)
            cast_done = casts[FC - 1]
            if split:
                assert keep is not None
                return casts, cast_done, functools.partial(
                    _ln_rest, casts, dsts, extra, bc_dst, slot_guard)
            return (_ln_rest(None, None, extra, bc_dst, slot_guard),
                    cast_done, casts)

        _ln_sums = [0, 0]

        def _ln_fc_rest(fc, dst, cast, slot_guard, extra):
            sl = fc % 2
            sq = ea(lambda eng, dst=dst, sl=sl: eng.activation(
                sqb[:, 512 * sl:512 * sl + 512], dst, AF.Square),
                [("dve", cast), ("pe", slot_guard[sl]), LDC])
            w_mm = [("dve", cast)]
            if fc == 0:
                w_mm += extra + [("dve", ln_state["stats_read"]),
                                 ("act", pw_act[6]), ("dve", pw_dve[6]),
                                 LDC]
            et(lambda eng, dst=dst, fc=fc: eng.matmul(
                P[6][0:1, :], on_sb[:, 0:1], dst,
                start=(fc == 0), stop=(fc == FC - 1)), w_mm)
            w_mm2 = [("act", sq)]
            if fc == 0:
                w_mm2 += extra + [("dve", ln_state["stats_read"]),
                                  ("act", pw_act[7]), ("dve", pw_dve[7])]
            r = et(lambda eng, sl=sl, fc=fc: eng.matmul(
                P[7][0:1, :], on_sb[:, 0:1],
                sqb[:, 512 * sl:512 * sl + 512],
                start=(fc == 0), stop=(fc == FC - 1)), w_mm2, inc=True)
            slot_guard[sl] = r
            if fc == FC - 1:
                _ln_sums[0] = r
                _ln_sums[1] = r

        def _ln_rest(casts, dsts, extra, bc_dst, slot_guard):
            if casts is not None:
                for fc in range(FC):
                    _ln_fc_rest(fc, dsts[fc], casts[fc], slot_guard, extra)
            pe_sum, pe_sq = _ln_sums
            xbf_war[0] = slot_guard[0]
            xbf_war[1] = slot_guard[1]
            ev(lambda eng: eng.tensor_scalar(
                rows[0:1, 0:512], P[6][0:1, :], -INVD, None, ALU.mult),
                [("pe", pe_sum), ("lnb", ln_state["row_war"])])
            ev(lambda eng: eng.tensor_scalar(
                rows[0:1, 1024:1536], P[7][0:1, :], INVD, None, ALU.mult),
                [("pe", pe_sq)])
            ev(lambda eng: eng.tensor_tensor(
                rows[0:1, 512:1024], rows[0:1, 0:512], rows[0:1, 0:512],
                ALU.mult))
            d_var = ev(lambda eng: eng.tensor_tensor(
                rows[0:1, 1024:1536], rows[0:1, 1024:1536],
                rows[0:1, 512:1024], ALU.subtract))
            ln_state["stats_read"] = d_var
            pw_dve[6] = d_var
            pw_dve[7] = d_var
            ea(lambda eng: eng.activation(
                rows[0:1, 1024:1536], rows[0:1, 1024:1536], AF.Ln,
                bias=eps_sb[0:1, :]), [("dve", d_var), LDC])
            a_exp = ea(lambda eng: eng.activation(
                rows[0:1, 512:1024], rows[0:1, 1024:1536], AF.Exp,
                scale=-0.5))
            slot = ln_state["slot"] % 4
            ln_state["slot"] += 1
            b_out = es(lambda eng, slot=slot: eng.dma_start(
                out=rowb[slot:slot + 1, :], in_=rows[0:1, 0:1024]),
                "lnb", [("act", a_exp), ("lnb", ln_state["row_war"])])
            dst = bc[:, 0:1024] if bc_dst is None else bc_dst
            bc_in = es(lambda eng, slot=slot, dst=dst: eng.dma_start(
                out=dst,
                in_=rowb[slot:slot + 1, :].partition_broadcast(128)),
                "lnb", [("lnb", b_out), ("dve", ln_state["bc_reader"]),
                        ("pool", ln_state["bc_reader_pool"])])
            ln_state["row_war"] = bc_in
            return bc_in

        xn_done = [0] * NT

        def ln1_stats(t):
            return ln_stats(
                lambda fc, t=t: x_sb[:, T * fc + 512 * t:
                                     T * fc + 512 * t + 512],
                extra_waits=[XL[t]])[0]

        def ln1_apply(t, bc_in):
            for fc in range(FC):
                ev(lambda eng, fc=fc, t=t: eng.tensor_tensor(
                    tmp[:, :], x_sb[:, T * fc + 512 * t:
                                    T * fc + 512 * t + 512],
                    bc[:, 0:512], ALU.add), [("lnb", bc_in)])
                ev(lambda eng, fc=fc, t=t: eng.tensor_tensor(
                    bfA[:, T * fc + 512 * t: T * fc + 512 * t + 512],
                    tmp[:, :], bc[:, 512:1024], ALU.mult))
            xn_done[t] = CNT["dve"]
            ln_state["bc_reader"] = CNT["dve"]

        kf_war = [0, 0]
        vr_war = [0, 0]
        kv_mm_last = [0]
        last_k_mm = [0]
        last_v_mm = [0]
        kv_add_last = [0]

        bc0 = ln1_stats(0)
        ln1_apply(0, bc0)
        ln1_pend = [None]

        def kv_chunk(c):
            t = c // 4
            if c % 4 == 2 and t + 1 < NT:
                ln1_pend[0] = ln1_stats(t + 1)
            if c % 4 == 3 and ln1_pend[0] is not None:
                ln1_apply(t + 1, ln1_pend[0])
                ln1_pend[0] = None
            sl_ = c % 2
            kfr_v = kfr_s[sl_]
            vr_v = vr_s[sl_]
            for ot in range(2):
                bank = ot
                for fc in range(FC):
                    w = []
                    if fc == 0:
                        w = [("swk", 16), ("dve", xn_done[t]),
                             ("act", pw_act[bank]), ("dve", pw_dve[bank]),
                             ("sbk", 16), LDC]
                    et(lambda eng, fc=fc, ot=ot, c=c, bank=bank: eng.matmul(
                        P[bank][:, :],
                        bfA[:, T * fc + 128 * c: T * fc + 128 * c + 128],
                        wa[:, 1024 * fc + 512 * ot:
                           1024 * fc + 512 * ot + 512],
                        start=(fc == 0), stop=False), w)
                mmk = et(lambda eng, ot=ot, bank=bank: eng.matmul(
                    P[bank][:, :], on_sb[0:1, 0:128],
                    bkv_sb[0:1, 512 * ot:512 * ot + 512],
                    start=False, stop=True), [], inc=True)
                last_k_mm[0] = mmk
                e = ea(lambda eng, ot=ot, c=c, bank=bank, kfr_v=kfr_v:
                       eng.activation(
                    kfr_v[:, 512 * ot:512 * ot + 512], P[bank][:, :],
                    AF.Relu, scale=dcol[:, c:c + 1]),
                    [("pe", mmk), ("pe", kf_war[sl_]), LDC])
                pw_act[bank] = e
            for ot in range(2):
                bank = 2 + ot
                for fc in range(FC):
                    w = []
                    if fc == 0:
                        w = [("swv", 16), ("dve", xn_done[t]),
                             ("act", pw_act[bank]), ("dve", pw_dve[bank])]
                    et(lambda eng, fc=fc, ot=ot, c=c, bank=bank: eng.matmul(
                        P[bank][:, :],
                        bfA[:, T * fc + 128 * c: T * fc + 128 * c + 128],
                        wa[:, 8192 + 1024 * fc + 512 * ot:
                           8192 + 1024 * fc + 512 * ot + 512],
                        start=(fc == 0), stop=False), w)
                mmv = et(lambda eng, ot=ot, bank=bank: eng.matmul(
                    P[bank][:, :], on_sb[0:1, 0:128],
                    bkv_sb[0:1, D + 512 * ot:D + 512 * ot + 512],
                    start=False, stop=True), [], inc=True)
                last_v_mm[0] = mmv
                e = ea(lambda eng, ot=ot, bank=bank, vr_v=vr_v:
                       eng.activation(
                    vr_v.rearrange("p (s c) -> p s c", c=130)[
                        :, 4 * ot:4 * ot + 4, 0:128],
                    P[bank][:, 0:512].rearrange("p (s c) -> p s c", c=128),
                    AF.Copy),
                    [("pe", mmv), ("pe", vr_war[sl_])])
                pw_act[bank] = e
            kevac_q[c] = CNT["act"]
            if c > 0:
                kv_pairs(c - 1)

        def kv_pairs(c):
            sl_ = c % 2
            kfr_v = kfr_s[sl_]
            vr_v = vr_s[sl_]
            kevac = kevac_q[c]
            for p in range(NP):
                bank = 4 + (p % 2)
                w = [("act", kevac), ("dve", pw_dve[bank])]
                if c <= 1 and p < 2:
                    w.append(("dve", d_ones))
                mm = et(lambda eng, p=p, bank=bank, kfr_v=kfr_v, vr_v=vr_v:
                        eng.matmul(
                    P[bank][:, 0:130],
                    kfr_v[:, 128 * p:128 * p + 128],
                    vr_v[:, 130 * p:130 * p + 130],
                    start=True, stop=True), w, inc=True)
                if c == 0:
                    a = ev(lambda eng, p=p, bank=bank: eng.tensor_copy(
                        kvacc[:, 130 * p:130 * p + 130], P[bank][:, 0:130]),
                        [("pe", mm)])
                else:
                    a = ev(lambda eng, p=p, bank=bank: eng.tensor_tensor(
                        kvacc[:, 130 * p:130 * p + 130],
                        kvacc[:, 130 * p:130 * p + 130],
                        P[bank][:, 0:130], ALU.add), [("pe", mm)])
                pw_dve[bank] = a
                kv_mm_last[0] = mm
            kf_war[sl_] = kv_mm_last[0]
            vr_war[sl_] = kv_mm_last[0]
            kv_add_last[0] = CNT["dve"]

        kevac_q = [0] * NCH
        for c in range(NCH):
            kv_chunk(c)
        kv_pairs(NCH - 1)

        cc_cast = ev(lambda eng: eng.tensor_copy(kvbd[:, :], kvacc[:, :]),
                     [("dve", kv_add_last[0]), ("pe", kv_mm_last[0])])
        eg(lambda eng: eng.dma_start(out=cc_in[:, :], in_=kvbd[:, :]),
           "g1s", [("dve", cc_cast)])

        def coll(eng):
            return eng.collective_compute(
                "AllReduce", ALU.add,
                replica_groups=[list(range(NCORE))],
                ins=[cc_in.ap().opt()],
                outs=[cc_out.ap().opt()])
        emit("gpsimd", coll, [("g1s", 16)], "cc", 1)
        eg(lambda eng: eng.dma_start(out=kvbd[:, :], in_=cc_out[:, :]),
           "g2s", [("cc", 1)])
        eg(functools.partial(wload, dst0=8192, src=wo), "swo",
           [("pe", last_v_mm[0])])

        tmp_war = [CNT["dve"]]
        last_q_mm = [0]
        for oc in range(FC):
            mmq_t = [0] * NT
            for t in range(NT):
                bank = 4 * (oc % 2) + t
                for fc in range(FC):
                    w = []
                    if fc == 0:
                        w = [("swq", 16),
                             ("act", pw_act[bank]), ("dve", pw_dve[bank])]
                    et(lambda eng, fc=fc, oc=oc, t=t, bank=bank: eng.matmul(
                        P[bank][:, :],
                        qh[:, 8192 + 1024 * oc + 128 * fc:
                           8192 + 1024 * oc + 128 * fc + 128],
                        bfA[:, T * fc + 512 * t: T * fc + 512 * t + 512],
                        start=(fc == 0), stop=(fc == FC - 1)), w,
                        inc=(fc == FC - 1))
                mmq_t[t] = CNT["pe"]
            last_q_mm[0] = mmq_t[NT - 1]
            for t in range(NT):
                bank = 4 * (oc % 2) + t
                e = ea(lambda eng, oc=oc, bank=bank: eng.activation(
                    tmp[:, :], P[bank][:, :], AF.Relu,
                    bias=bq_sb[:, oc:oc + 1]),
                    [("pe", mmq_t[NT - 1]), ("dve", tmp_war[0]), LDC])
                m = ev(lambda eng, oc=oc, t=t: eng.tensor_tensor(
                    qh[:, T * oc + 512 * t: T * oc + 512 * t + 512],
                    tmp[:, :], decbc_v[:, 512 * t:512 * t + 512], ALU.mult),
                    [("act", e)])
                tmp_war[0] = m
                pw_act[bank] = e
        qf_done = CNT["dve"]

        kbd3 = kvbd.ap().rearrange("p (s c) -> p s c", c=130)
        ev(lambda eng: eng.memset(kbd3[0:64, :, 64:128], 0.0), [("g2s", 16)])
        ev(lambda eng: eng.memset(kbd3[64:128, :, 0:64], 0.0))
        ev(lambda eng: eng.memset(kbd3[64:128, :, 128:129], 0.0))
        kvbd_done = ev(lambda eng: eng.memset(kbd3[0:64, :, 129:130], 0.0))

        odv_war = [0] * NP
        trev = [[0] * NCH for _ in range(NP)]

        def out_chunk(c, wo_qtm1=None):
            for half in range(2):
                mm_out = [0] * 4
                for j in range(4):
                    p = 4 * half + j
                    bank = j
                    w = [("dve", kvbd_done), ("act", pw_act[bank]),
                         ("dve", pw_dve[bank])]
                    mm_out[j] = et(lambda eng, c=c, p=p, bank=bank:
                                   eng.matmul(
                        P[bank][:, 0:130],
                        qh[:, T * p + 128 * c: T * p + 128 * c + 128],
                        kvbd[:, 130 * p:130 * p + 130],
                        start=True, stop=True), w, inc=True)
                for j in range(4):
                    p = 4 * half + j
                    ev(lambda eng, p=p, bank=j: eng.tensor_scalar(
                        recip.ap()[:, 2 * p:2 * p + 2],
                        P[bank][:, 128:130],
                        ATT_EPS, None, ALU.add), [("pe", mm_out[j])])
                ev(lambda eng, half=half: eng.reciprocal(
                    recip[:, 8 * half:8 * half + 8],
                    recip[:, 8 * half:8 * half + 8]))
                if wo_qtm1 is not None:
                    wo_group(wo_qtm1, 2 * (c % 4) + half)
                for j in range(4):
                    p = 4 * half + j
                    bank = j
                    dvm = ev(lambda eng, p=p, bank=bank: eng.tensor_tensor(
                        odv_v.rearrange("p (s c) -> p s c", c=64)[
                            :, 2 * p:2 * p + 2, :],
                        P[bank][:, 0:128].rearrange(
                            "p (s c) -> p s c", c=64),
                        recip.ap()[:, 2 * p:2 * p + 2].unsqueeze(2)
                        .broadcast_to([128, 2, 64]),
                        ALU.mult), [("pe", odv_war[p])])
                    pw_dve[bank] = dvm
                    tb = p % 2
                    trmm = et(lambda eng, p=p, tb=tb: eng.transpose(
                        TRB[tb], odv_v[:, 128 * p:128 * p + 128],
                        id_sb[:, :]),
                        [("dve", dvm), ("act", pw_act[6 + tb]),
                         ("dve", pw_dve[6 + tb]), LDC], inc=True)
                    odv_war[p] = trmm
                    tev = ea(lambda eng, p=p, c=c, tb=tb: eng.copy(
                        bfA[:, T * p + 128 * c: T * p + 128 * c + 128],
                        TRB[tb]), [("pe", trmm)])
                    pw_act[6 + tb] = tev
                    trev[p][c] = tev

        def w1load(eng, hg, slot):
            return eng.dma_start(
                out=wa.ap()[:, 2048 * slot:2048 * slot + 2048].rearrange(
                    "p (c h) -> p c h", c=FC),
                in_=w1.ap().rearrange("(c p) h -> p c h", p=128)[
                    :, :, 256 * hg:256 * hg + 256])

        def w2load(eng, wg, slot):
            return eng.dma_start(
                out=wa.ap()[:, 8192 + 4096 * slot:
                            8192 + 4096 * slot + 4096]
                .rearrange("p (c e) -> p c e", c=4),
                in_=w2.ap().rearrange("(c p) e -> p c e", p=128)[
                    :, 4 * wg:4 * wg + 4, :])

        w1_ring_war = [last_k_mm[0]] * 4
        w2_ring_war = [0, 0]
        W1SEM = ("w1a", "w1b", "w1c", "w1d")
        w1_pre = [0] * 4
        for u in range(4):
            w1_pre[u] = eg(functools.partial(w1load, hg=u, slot=u),
                           W1SEM[u], [("pe", last_k_mm[0])])

        xn2_ready_q = [0] * NT
        w1_last_mm = [0]
        ln3_bc = [0]
        x_upd2 = [0] * NT

        def emit_ln3_stats(qt):
            ln3_extra = []
            if qt == 0:
                ln3_extra += [("pe", kv_mm_last[0]), ("dve", qf_done)]
            ln3_bc[0] = ln_stats(
                lambda fc, qt=qt: x_sb[:, T * fc + 512 * qt:
                                       T * fc + 512 * qt + 512],
                extra_waits=ln3_extra,
                fc_wait=lambda fc, qt=qt: [("pool", x_upd_fc[qt][fc])],
                bc_dst=(rows[:, 0:1024] if qt == 0 else None))[0]

        xn2_fc = [0] * FC

        def emit_ln3_apply(qt, fcs=tuple(range(FC))):
            srcM = rows[:, 0:512] if qt == 0 else bc[:, 0:512]
            srcA = rows[:, 512:1024] if qt == 0 else bc[:, 512:1024]
            for fc in fcs:
                w = [("lnb", ln3_bc[0])]
                if fc == 0 and qt > 0:
                    w.append(("pe", w1_last_mm[0]))
                ev(lambda eng, fc=fc, qt=qt, srcM=srcM: eng.tensor_tensor(
                    tmp[:, :], x_sb[:, T * fc + 512 * qt:
                                    T * fc + 512 * qt + 512],
                    srcM, ALU.add), w)
                xn2_fc[fc] = ev(lambda eng, fc=fc, srcA=srcA:
                                eng.tensor_tensor(
                    xn2_v[:, 512 * fc:512 * fc + 512],
                    tmp[:, :], srcA, ALU.mult))
            xn2_ready_q[qt] = xn2_fc[FC - 1]
            ln_state["bc_reader"] = CNT["dve"]

        yA_war = [0]
        ya_rel = [[0] * FC]
        x_upd_fc = [[0] * FC for _ in range(NT)]
        last_wo_mm = [0]
        wo_yev = [[0] * FC for _ in range(NT)]
        wo_mm_q = [0] * NT

        def wo_group(qtm1, oc):
            bank = 4 + (oc % 2)
            for ec in range(FC):
                w = [("act", trev[ec][4 * qtm1 + 3])]
                if ec == 0:
                    w += [("swo", 16), ("act", pw_act[bank]),
                          ("dve", pw_dve[bank])]
                et(lambda eng, oc=oc, ec=ec, qtm1=qtm1, bank=bank:
                   eng.matmul(
                       P[bank][:, :],
                       wa[:, 8192 + 1024 * ec + 128 * oc:
                          8192 + 1024 * ec + 128 * oc + 128],
                       bfA[:, T * ec + 512 * qtm1:
                           T * ec + 512 * qtm1 + 512],
                       start=(ec == 0), stop=(ec == FC - 1)), w,
                   inc=(ec == FC - 1))
            mmo = CNT["pe"]
            last_wo_mm[0] = mmo
            wo_mm_q[qtm1] = mmo
            e = ea(lambda eng, oc=oc, bank=bank: eng.activation(
                yA[:, 512 * oc:512 * oc + 512], P[bank][:, :],
                AF.Identity, bias=bo_sb[:, oc:oc + 1]),
                [("pe", mmo), ("dve", ya_rel[0][oc]), LDC])
            pw_act[bank] = e
            wo_yev[qtm1][oc] = e

        ln2_pend = [None]

        def ln2_applies(qt, ybf, bc_in):
            op = ep if qt < 2 else ev
            for fc in range(FC):
                op(lambda eng, fc=fc, ybf=ybf: eng.tensor_tensor(
                    tmp2, ybf(fc), bc[:, 0:512], ALU.add),
                    [("lnb", bc_in)] + ([("pool", x_upd2[1])]
                                        if qt == 2 and fc == 0 else []))
                op(lambda eng: eng.tensor_tensor(
                    tmp2, tmp2, bc[:, 512:1024], ALU.mult))
                op(lambda eng, fc=fc, qt=qt: eng.tensor_tensor(
                    x_sb[:, T * fc + 512 * qt: T * fc + 512 * qt + 512],
                    x_sb[:, T * fc + 512 * qt: T * fc + 512 * qt + 512],
                    tmp2, ALU.add))
                x_upd_fc[qt][fc] = CNT["pool"] if qt < 2 else 0
            x_upd2[qt] = CNT["pool"] if qt < 2 else 0
            ln_state["bc_reader"] = CNT["dve"]
            ln_state["bc_reader_pool"] = CNT["pool"]

        def emit_ln2(qt, split=False):
            ybf = (lambda fc, qt=qt:
                   bfA[:, T * fc + 512 * qt: T * fc + 512 * qt + 512])
            r = ln_stats(
                lambda fc: yA[:, 512 * fc:512 * fc + 512],
                keep=ybf, extra_waits=[("pe", wo_mm_q[qt])],
                fc_wait=lambda fc, qt=qt: [("act", wo_yev[qt][fc])],
                split=split)
            if split:
                casts, cast_done, rest = r
                yA_war[0] = cast_done
                ya_rel[0] = casts
                ln2_pend[0] = (qt, ybf, rest)
                return
            bc_in, cast_done, casts = r
            yA_war[0] = cast_done
            ya_rel[0] = casts
            ln2_applies(qt, ybf, bc_in)

        def finish_ln2():
            qt, ybf, rest = ln2_pend[0]
            ln2_pend[0] = None
            ln2_applies(qt, ybf, rest())

        for qt in range(NT):
            for c in range(4 * qt, 4 * qt + 4):
                out_chunk(c, wo_qtm1=(qt - 1 if qt >= 1 else None))
                if qt == 3 and c == 13:
                    emit_ln3_apply(0)
                elif qt == 3 and c == 14:
                    finish_ln2()
            if qt == 1:
                emit_ln2(0)
            elif qt == 2:
                emit_ln3_stats(0)
                emit_ln2(1, split=True)
            elif qt == 3:
                emit_ln2(2)
        for oc in range(FC):
            wo_group(3, oc)
        emit_ln2(3)

        h_war = [0] * HC
        yA2_war = [0] * FC
        h_gelu_cnt = [0] * HC

        ln4_pend = [None]

        def ln4_applies(qt, ybf4, bc_in):
            for fc in range(FC):
                ev(lambda eng, fc=fc, ybf4=ybf4: eng.tensor_tensor(
                    tmp[:, :], ybf4(fc),
                    bc[:, 0:512], ALU.add), [("lnb", bc_in)])
                ev(lambda eng: eng.tensor_tensor(
                    tmp[:, :], tmp[:, :], bc[:, 512:1024], ALU.mult))
                xu = ev(lambda eng, fc=fc, qt=qt: eng.tensor_tensor(
                    x_sb[:, T * fc + 512 * qt: T * fc + 512 * qt + 512],
                    x_sb[:, T * fc + 512 * qt: T * fc + 512 * qt + 512],
                    tmp[:, :], ALU.add))
                es(lambda eng, fc=fc, qt=qt: eng.dma_start(
                    out=yout[128 * fc:128 * fc + 128,
                             512 * qt:512 * qt + 512],
                    in_=x_sb[:, T * fc + 512 * qt:
                             T * fc + 512 * qt + 512]),
                    "out", [("dve", xu)])
            ln_state["bc_reader"] = CNT["dve"]

        def finish_ln4():
            qt, ybf4, rest = ln4_pend[0]
            ln4_pend[0] = None
            ln4_applies(qt, ybf4, rest())

        for qt in range(NT):
            if qt == 0:
                w2_ring_war = [last_wo_mm[0], last_wo_mm[0]]

            w2_pre = [0, 0]
            for hg in range(16):
                slot = hg % 4
                if hg < 4:
                    gld = w1_pre[hg]
                else:
                    gld = eg(functools.partial(w1load, hg=hg, slot=slot),
                             W1SEM[slot], [("pe", w1_ring_war[slot])])
                if hg < 2:
                    s2 = "w2a" if hg == 0 else "w2b"
                    w2_pre[hg] = eg(
                        functools.partial(w2load, wg=hg, slot=hg),
                        s2, [("pe", w2_ring_war[hg])])
                if hg == 2 and ln4_pend[0] is not None:
                    finish_ln4()
                if hg == 11 and qt + 1 < NT:
                    emit_ln3_stats(qt + 1)
                for hh2 in range(2):
                    hh = 2 * hg + hh2
                    bank = hh % 2
                    for fc in range(FC):
                        w = [("dve", xn2_fc[fc])]
                        if fc == 0:
                            w += [(W1SEM[slot], gld),
                                  ("act", pw_act[bank]),
                                  ("dve", pw_dve[bank])]
                        et(lambda eng, fc=fc, hh2=hh2, slot=slot, bank=bank:
                           eng.matmul(
                               P[bank][:, :],
                               wa[:, 2048 * slot + 256 * fc + 128 * hh2:
                                  2048 * slot + 256 * fc + 128 * hh2 + 128],
                               xn2_v[:, 512 * fc:512 * fc + 512],
                               start=(fc == 0), stop=(fc == FC - 1)), w,
                           inc=(fc == FC - 1))
                    mmh = CNT["pe"]
                    w1_ring_war[slot] = mmh
                    e = ea(lambda eng, hh=hh, bank=bank: eng.activation(
                        qh[:, 512 * hh:512 * hh + 512], P[bank][:, :],
                        AF.Gelu, bias=b1_sb[:, hh:hh + 1]),
                        [("pe", mmh), ("pe", h_war[hh]), LDC])
                    pw_act[bank] = e
                    h_gelu_cnt[hh] = e

            w1_last_mm[0] = CNT["pe"]
            if qt + 1 < NT:
                for u in range(4):
                    w1_pre[u] = eg(
                        functools.partial(w1load, hg=u, slot=u),
                        W1SEM[u], [("pe", w1_ring_war[u])])
                emit_ln3_apply(qt + 1)

            first_hh = [True] * FC
            last_w2_mm = [0] * FC
            for wg in range(FC):
                slot = wg % 2
                sem = "w2a" if slot == 0 else "w2b"
                if wg < 2:
                    gld = w2_pre[wg]
                else:
                    gld = eg(functools.partial(w2load, wg=wg, slot=slot),
                             sem, [("pe", w2_ring_war[slot])])
                for hh4 in range(4):
                    hh = 4 * wg + hh4
                    ecs = (2, 3, 4, 5, 6, 7, 0, 1) if hh == 0 else range(FC)
                    for ec in ecs:
                        w = [("act", h_gelu_cnt[hh]), (sem, gld)]
                        if first_hh[ec]:
                            w += [("act", pw_act[ec]), ("dve", pw_dve[ec]),
                                  ("act", yA2_war[ec])]
                            first_hh[ec] = False
                        do_inc = (hh == HC - 1 or
                                  (hh4 == 3 and ec == FC - 1))
                        r = et(
                            lambda eng, hh=hh, hh4=hh4, ec=ec, slot=slot:
                            eng.matmul(
                                P[ec][:, :],
                                wa[:, 8192 + 4096 * slot + 1024 * hh4
                                   + 128 * ec:
                                   8192 + 4096 * slot + 1024 * hh4
                                   + 128 * ec + 128],
                                qh[:, 512 * hh:512 * hh + 512],
                                start=(hh == 0), stop=(hh == HC - 1)), w,
                            inc=do_inc)
                        if hh == HC - 1:
                            last_w2_mm[ec] = r
                w2_ring_war[slot] = CNT["pe"]
            for ec in range(FC):
                e = ea(lambda eng, ec=ec: eng.activation(
                    yA[:, 512 * ec:512 * ec + 512], P[ec][:, :],
                    AF.Identity, bias=b2_sb[:, ec:ec + 1]),
                    [("pe", last_w2_mm[ec]), ("dve", yA_war[0]), LDC])
                yA2_war[ec] = e
                pw_act[ec] = e
            for hh in range(HC):
                h_war[hh] = CNT["pe"]

            ybf4 = (lambda fc, qt=qt:
                    bfA[:, T * fc + 512 * qt: T * fc + 512 * qt + 512])
            if qt + 1 < NT:
                casts, cast_done, rest = ln_stats(
                    lambda fc: yA[:, 512 * fc:512 * fc + 512],
                    keep=ybf4, fc_wait=lambda fc: [("act", yA2_war[fc])],
                    split=True)
                yA_war[0] = cast_done
                ln4_pend[0] = (qt, ybf4, rest)
            else:
                bc_in, cast_done, _ = ln_stats(
                    lambda fc: yA[:, 512 * fc:512 * fc + 512],
                    keep=ybf4, fc_wait=lambda fc: [("act", yA2_war[fc])])
                yA_war[0] = cast_done
                ln4_applies(qt, ybf4, bc_in)

        emit("sync", lambda eng: eng.nop(), [("out", CNT["out"])])

        block = en(nc.Block())

        def runner(name):
            def run(eng):
                for fn, waits, inc, amt in SCHED[name]:
                    for s, v in waits:
                        eng.wait_ge(SEM[s], v)
                    inst = fn(eng)
                    if inc is not None:
                        inst.then_inc(SEM[inc], amt)
            return run

        block.sync(runner("sync"))
        block.gpsimd(runner("gpsimd"))
        block.tensor(runner("tensor"))
        block.vector(runner("vector"))
        block.scalar(runner("scalar"))

    return nc


def kernel(L0_attn_x, L0_attn_W_q, L0_attn_b_q, L0_attn_W_k, L0_attn_b_k,
           L0_attn_W_v, L0_attn_b_v, L0_attn_W_o, L0_attn_b_o,
           L0_ffn_w1, L0_ffn_b1, L0_ffn_w2, L0_ffn_b2):
    x = np.asarray(L0_attn_x, dtype=np.float32)
    wq_h = np.ascontiguousarray(np.asarray(L0_attn_W_q, np.float32).T)
    wk_h = np.ascontiguousarray(np.asarray(L0_attn_W_k, np.float32).T)
    wv_h = np.ascontiguousarray(np.asarray(L0_attn_W_v, np.float32).T)
    wo_h = np.ascontiguousarray(np.asarray(L0_attn_W_o, np.float32).T)
    w1_h = np.ascontiguousarray(np.asarray(L0_ffn_w1, np.float32))
    w2_h = np.ascontiguousarray(np.asarray(L0_ffn_w2, np.float32))
    bq_h = np.ascontiguousarray(
        np.asarray(L0_attn_b_q, np.float32).reshape(8, 128).T)
    bo_h = np.ascontiguousarray(
        np.asarray(L0_attn_b_o, np.float32).reshape(8, 128).T)
    b1_h = np.ascontiguousarray(
        np.asarray(L0_ffn_b1, np.float32).reshape(32, 128).T)
    b2_h = np.ascontiguousarray(
        np.asarray(L0_ffn_b2, np.float32).reshape(8, 128).T)
    bkv_h = np.ascontiguousarray(np.concatenate(
        [np.asarray(L0_attn_b_k, np.float32),
         np.asarray(L0_attn_b_v, np.float32)])[None, :])
    ident_h = np.eye(128, dtype=BF)
    ones_h = np.ones((128, 128), dtype=BF)
    eps_h = np.full((128, 1), LN_EPS, np.float32)

    nc = build_program()
    in_maps = []
    for c in range(NCORE):
        dec = np.exp(-(np.arange(T, dtype=np.float64) + T * c)
                     / float(SEQ)).astype(np.float32)
        decbc_h = np.ascontiguousarray(
            np.broadcast_to(dec[None, :], (128, T))).astype(BF)
        deccol_h = np.ascontiguousarray(dec.reshape(NCH, 128).T)
        xs = np.ascontiguousarray(x[0, T * c:T * (c + 1), :].T)
        in_maps.append({
            "xin": xs, "wq": wq_h, "wk": wk_h, "wv": wv_h, "wo": wo_h,
            "w1": w1_h, "w2": w2_h, "bq": bq_h, "bo": bo_h,
            "b1": b1_h, "b2": b2_h, "bkv": bkv_h,
            "decbc": decbc_h, "deccol": deccol_h,
            "ident": ident_h, "onesin": ones_h, "epsin": eps_h,
        })

    res = run_bass_kernel_spmd(nc, in_maps, core_ids=list(range(NCORE)))
    out = np.empty((1, SEQ, D), np.float32)
    for c in range(NCORE):
        out[0, T * c:T * (c + 1), :] = res.results[c]["y"].T
    return out
